# version 10
# baseline (speedup 1.0000x reference)
import math
import contextlib
import numpy as np
import ml_dtypes
import concourse.bass as bass
import concourse.mybir as mybir
from concourse.bass_utils import run_bass_kernel_spmd

F32 = mybir.dt.float32
BF16 = mybir.dt.bfloat16
AF = mybir.ActivationFunctionType
ALU = mybir.AluOpType
AX = mybir.AxisListType

NCORE = 8
NSEQ = 4
S = 2048
D = 1024
NTC = S // 128
NDC = D // 128
HD = 64
NHM = 16
NHL = 8
INNER = 512
WIN = 512
HID = 2730
NJ = 22
TAILJ = 42
EPS_LN = 1e-5
XTW = 2064

AR_COLS = 88640
TAB0 = AR_COLS - 4 * 2048


def make_consts():
    c = {}
    c["IDENT"] = np.eye(128, dtype=np.float32).astype(ml_dtypes.bfloat16)
    c["IDF32"] = np.eye(128, dtype=np.float32)
    o2 = np.zeros((128, 2), dtype=np.float32)
    o2[0:64, 0] = 1.0
    o2[64:128, 1] = 1.0
    c["ONES2"] = o2.astype(ml_dtypes.bfloat16)
    c["ONES1"] = np.ones((128, 1), dtype=np.float32).astype(ml_dtypes.bfloat16)
    r = np.arange(128)[:, None]
    i = np.arange(128)[None, :]
    mc = np.where(r <= i, 0.0, -30000.0).astype(np.float32)
    mu = np.where(r >= i, 0.0, -30000.0).astype(np.float32)
    c["MASKC"] = mc.astype(ml_dtypes.bfloat16)
    c["MASKU"] = mu.astype(ml_dtypes.bfloat16)
    half = np.arange(0, HD, 2, dtype=np.float32)
    inv_freq = 1.0 / 10000.0 ** (half / HD)
    t = np.arange(S, dtype=np.float32)
    freqs = t[:, None] * inv_freq[None, :]
    freqs = np.concatenate([freqs, freqs], axis=-1)
    base = (half + 0.4 * HD) / (1.4 * HD)
    power = (t - S // 2) / float(WIN // 2)
    scale = base[None, :] ** power[:, None]
    scale = np.concatenate([scale, scale], axis=-1)
    cos = np.cos(freqs)
    sin = np.sin(freqs)
    sgn = np.ones((S, HD), dtype=np.float32)
    sgn[:, 0:32] = -1.0
    csq = (cos * scale).T
    snq = (sin * scale * sgn).T
    csk = (cos / scale).T
    snk = (sin / scale * sgn).T
    dup = lambda a: np.concatenate([a, a], axis=0).astype(np.float32)
    c["CSQ2"] = dup(csq).astype(ml_dtypes.bfloat16)
    c["SNQ2"] = dup(snq).astype(ml_dtypes.bfloat16)
    c["CSK2"] = dup(csk).astype(ml_dtypes.bfloat16)
    c["SNK2"] = dup(snk).astype(ml_dtypes.bfloat16)
    return c


class Sem:
    def __init__(self, name, step=1):
        self.name, self.step, self.n, self.h = name, step, 0, None


class Prog:
    ENGS = ("t", "v", "s", "g", "y")

    def __init__(self):
        self.q = {k: [] for k in self.ENGS}
        self.waited = {}

    def wait(self, eng, tok):
        if tok is None:
            return
        sem, val = tok
        if val <= 0:
            return
        k = (eng, sem.name)
        if self.waited.get(k, -1) >= val:
            return
        self.waited[k] = val
        self.q[eng].append(lambda e, s=sem, v=val: e.wait_ge(s.h, v))

    def ins(self, eng, fn, inc=None, waits=()):
        for w in waits:
            self.wait(eng, w)
        if inc is not None:
            inc.n += inc.step
            self.q[eng].append(
                lambda e, f=fn, s=inc: f(e).then_inc(s.h, s.step))
            return (inc, inc.n)
        self.q[eng].append(fn)
        return None


def build_core(dump=()):
    nc = bass.Bass("TRN2", target_bir_lowering=False)
    P = Prog()
    dump = set(dump)

    di = {}
    def inp(name, shape, dt=F32):
        di[name] = nc.dram_tensor(name, list(shape), dt, kind="ExternalInput").ap()
        return di[name]

    xin = inp("xin", [NSEQ, S, D])
    qkvw = inp("qkvw", [3 * D, D])
    qkvb = inp("qkvb", [3 * D])
    projw = inp("projw", [D, D])
    projb = inp("projb", [D])
    fwqkv = inp("fwqkv", [2, D, 3 * INNER])
    fqs = inp("fqs", [2, HD])
    fks = inp("fks", [2, HD])
    fwo = inp("fwo", [2, INNER, D])
    flng = inp("flng", [2, D])
    flnb = inp("flnb", [2, D])
    fw1 = inp("fw1", [2, D, 2 * HID])
    fb1 = inp("fb1", [2, 2 * HID])
    fw2 = inp("fw2", [2, HID, D])
    fb2 = inp("fb2", [2, D])
    inp("IDENT", [128, 128], BF16)
    inp("IDF32", [128, 128])
    inp("ONES2", [128, 2], BF16)
    inp("ONES1", [128, 1], BF16)
    inp("MASKC", [128, 128], BF16)
    inp("MASKU", [128, 128], BF16)
    inp("CSQ2", [128, S], BF16)
    inp("SNQ2", [128, S], BF16)
    inp("CSK2", [128, S], BF16)
    inp("SNK2", [128, S], BF16)

    out = nc.dram_tensor("out", [NSEQ, S, D], F32, kind="ExternalOutput").ap()
    xpa = nc.dram_tensor("xpa", [S, D], F32, kind="ExternalOutput" if "xpa" in dump
                         else "Internal").ap()
    dbg = {}
    for tag in dump:
        if tag == "xpa":
            continue
        shape = {
            "XT": [NDC * 128, XTW], "KT": [NDC * 128, S], "V65": [NTC * 128, 1040],
            "QT": [NDC * 128, 512], "OQ": [NDC * 128, 512],
            "QT2": [4 * 128, S], "KT2": [4 * 128, S], "V65L": [NTC * 128, 520],
            "OQL": [4 * 128, 512], "XB": [NTC * 128, D], "HN": [NDC * 128, XTW],
            "M": [NJ * 128, 512], "XB2": [NTC * 128, D],
            "RSUM": [64, 512], "XWIN": [512, D], "STT2": [128, 48],
            "Q2R": [128, S], "RBC": [128, S], "LMU": [128, S], "LRS": [128, S],
            "LROWS": [2, S], "XTR": [NDC * 128, XTW],
        }[tag]
        dbg[tag] = nc.dram_tensor("dbg_" + tag, shape, BF16 if tag not in
                                  ("XB", "XB2", "RSUM", "XWIN", "STT2", "Q2R", "RBC", "LMU", "LRS", "LROWS") else F32,
                                  kind="ExternalOutput").ap()

    es = contextlib.ExitStack()
    with es:
        AR = es.enter_context(nc.sbuf_tensor("AR", [128, AR_COLS], BF16))
        ARF = AR.bitcast(F32)
        XT = es.enter_context(nc.sbuf_tensor("XT", [128, NDC * XTW], BF16))
        IDENT = es.enter_context(nc.sbuf_tensor("sIDENT", [128, 128], BF16))
        IDF32 = es.enter_context(nc.sbuf_tensor("sIDF32", [128, 128], F32))
        ONES2 = es.enter_context(nc.sbuf_tensor("sONES2", [128, 2], BF16))
        ONES1 = es.enter_context(nc.sbuf_tensor("sONES1", [128, 1], BF16))
        MASKC = es.enter_context(nc.sbuf_tensor("sMASKC", [128, 128], BF16))
        MASKU = es.enter_context(nc.sbuf_tensor("sMASKU", [128, 128], BF16))
        QKVB = es.enter_context(nc.sbuf_tensor("QKVB", [128, 24], F32))
        LNG = es.enter_context(nc.sbuf_tensor("LNG", [128, 16], F32))
        LNB = es.enter_context(nc.sbuf_tensor("LNB", [128, 16], F32))
        B1A = es.enter_context(nc.sbuf_tensor("B1A", [128, 44], F32))
        B1G = es.enter_context(nc.sbuf_tensor("B1G", [128, 44], F32))
        QS8 = es.enter_context(nc.sbuf_tensor("QS8", [128, 2], F32))
        KSC = es.enter_context(nc.sbuf_tensor("KSC", [128, 2], F32))
        STAT = es.enter_context(nc.sbuf_tensor("STAT", [128, 128], F32))
        PS = [es.enter_context(nc.psum_tensor(f"PS{i}", [128, 512], F32))
              for i in range(7)]
        PSB = [p.bitcast(BF16) for p in PS]
        TRANS = es.enter_context(nc.psum_tensor("TRANS", [128, 1024], BF16))

        sems = []
        def mksem(name, step=1):
            s_ = Sem(name, step)
            sems.append(s_)
            return s_
        TE = mksem("TE")
        VE = mksem("VE")
        SE = mksem("SE")
        QG = mksem("QG", 16)
        QY = mksem("QY", 16)
        QX = mksem("QX", 16)
        QW = [mksem(f"QW{i}", 16) for i in range(4)]
        for s_ in sems:
            s_.h = es.enter_context(nc.semaphore(s_.name))

        def bfv(col, n):
            return AR[:, col:col + n]

        def f32v(col, n):
            return ARF[:, col:col + n]

        A_KT = 0
        A_V65 = 16384
        A_WQT = 33024
        A_WPT = 41216
        A_QTL = 49408
        A_OQ = 53504
        A_PR = 57600
        A_XW = 60672
        A_SQ = 68864
        A_WB = 70912
        A_BV = 72960
        A_PB = 75008
        A_RB = 77056
        A_XB2 = 78080
        assert A_XB2 + 2048 <= TAB0

        B_X = 0
        B_QT2 = 33024
        B_KT2 = 41216
        B_V65 = 49408
        B_PR = 57728
        B_OQL = 60800
        B_Q2R = 62848
        B_SQB = 66944
        B_WB = 68992
        B_RB = 73088
        B_RBC = 74112
        assert B_RBC + 4096 <= TAB0
        L_MBC = 32768
        L_RBC = 36864
        L_TMP = 40960
        L_SQT = 45056
        L_SQT2 = 72704
        B_XBN = 74112
        M_M = 32768
        M_WAG = 44032
        M_W2 = 48128
        M_B2B = 70656
        M_GSC = 72704
        M_XPW = 32768
        assert M_W2 + 22528 <= M_B2B
        assert M_B2B + 2048 <= L_SQT2 + 1024 + 2048

        CSQ2 = bfv(TAB0 + 0 * S, S)
        SNQ2 = bfv(TAB0 + 1 * S, S)
        CSK2 = bfv(TAB0 + 2 * S, S)
        SNK2 = bfv(TAB0 + 3 * S, S)

        def dma(q, out_ap, in_ap, sem, waits=()):
            return P.ins(q, lambda e: e.dma_start(out=out_ap, in_=in_ap),
                         inc=sem, waits=waits)

        def dma_slow(q, out_ap, in_ap, sem, waits=()):
            return P.ins(q, lambda e: e.dma_start(
                out=out_ap, in_=in_ap, allow_slow_non_contiguous=True),
                inc=sem, waits=waits)

        def mm(out_ap, lhsT, rhs, start, stop, inc=None, waits=(), tp=None):
            return P.ins("t", lambda e: e.matmul(
                out_ap, lhsT, rhs, start=start, stop=stop, tile_position=tp),
                inc=inc, waits=waits)

        def transp(out_ap, in_ap, ident, inc=None, waits=()):
            return P.ins("t", lambda e: e.transpose(out_ap, in_ap, ident),
                         inc=inc, waits=waits)

        def act(out_ap, in_ap, func, bias=None, scale=None, accum=None,
                inc=None, waits=()):
            kw = {}
            if bias is not None:
                kw["bias"] = bias
            if scale is not None:
                kw["scale"] = scale
            if accum is not None:
                kw["accum_out"] = accum
            return P.ins("s", lambda e: e.activation(out_ap, in_ap, func, **kw),
                         inc=inc, waits=waits)

        def vop(name, *args, inc=None, waits=()):
            return P.ins("v", lambda e: getattr(e, name)(*args),
                         inc=inc, waits=waits)

        bank = [{"ready": None, "free": None} for _ in range(7)]
        trans_tok = {"ready": None, "free": None}
        sq_fence = [None]

        cwait = []
        for nm, tgt in [("IDENT", IDENT), ("IDF32", IDF32), ("ONES2", ONES2),
                        ("ONES1", ONES1), ("MASKC", MASKC), ("MASKU", MASKU)]:
            cwait.append(dma("g", tgt[:, :], di[nm], QG))
        for nm, col in [("CSQ2", TAB0), ("SNQ2", TAB0 + S),
                        ("CSK2", TAB0 + 2 * S), ("SNK2", TAB0 + 3 * S)]:
            cwait.append(dma("g", bfv(col, S), di[nm], QG))
        cwait.append(dma_slow("g", QKVB[:, 0:24],
                              di["qkvb"].rearrange("(c p) -> p c", p=128), QG))
        for l in range(2):
            cwait.append(dma_slow("g", LNG[:, l * 8:(l + 1) * 8],
                                  di["flng"][l].rearrange("(c p) -> p c", p=128), QG))
            cwait.append(dma_slow("g", LNB[:, l * 8:(l + 1) * 8],
                                  di["flnb"][l].rearrange("(c p) -> p c", p=128), QG))
            for j in range(NJ):
                na = 128 if j < NJ - 1 else TAILJ
                cwait.append(dma("g", B1A[0:na, l * NJ + j:l * NJ + j + 1],
                                 di["fb1"][l, j * 128:j * 128 + na].rearrange("(p one) -> p one", one=1), QG))
                cwait.append(dma("g", B1G[0:na, l * NJ + j:l * NJ + j + 1],
                                 di["fb1"][l, HID + j * 128:HID + j * 128 + na]
                                 .rearrange("(p one) -> p one", one=1), QG))
            for v_, src in [(QS8, "fqs"), (KSC, "fks")]:
                cwait.append(dma("g", v_[0:64, l:l + 1],
                                 di[src][l].rearrange("(p one) -> p one", one=1), QG))
                cwait.append(dma("g", v_[64:128, l:l + 1],
                                 di[src][l].rearrange("(p one) -> p one", one=1), QG))
        const_tok = cwait[-1]
        t_qs = vop("tensor_scalar_mul", QS8[:, :], QS8[:, :], 8.0,
                   inc=VE, waits=[const_tok])
        prolog_tok = t_qs

        def build_bias_bcasts(gate):
            toks = []
            last = gate
            for dst, src in ((A_BV, ("qkvb", 2 * D)), (A_PB, ("projb", 0))):
                nm, off = src
                for half in range(2):
                    tk1 = dma("y", ARF[0:1, A_RB // 2:A_RB // 2 + 512],
                              di[nm][off + half * 512: off + (half + 1) * 512]
                              .rearrange("(one x) -> one x", one=1), QY,
                              waits=[last] if last else ())
                    for b32 in range(4):
                        last = vop(
                            "stream_shuffle",
                            f32v(dst // 2 + half * 512, 512)[b32 * 32:(b32 + 1) * 32, :],
                            ARF[0:32, A_RB // 2:A_RB // 2 + 512], [0] * 32,
                            inc=VE, waits=[tk1])
                    toks.append(last)
            return toks[-1]

        xtpad_tok = None
        for dc in range(8):
            xtpad_tok = P.ins("v", lambda e, c=dc: e.memset(
                XT[:, c * XTW:c * XTW + 1], 0.0), inc=VE if dc == 7 else None)

        def xt_block(dc):
            return XT[:, dc * XTW: (dc + 1) * XTW]

        def build_T_from_nat(nat_tiles, dst_block_fn, ident, nat_tok_fn,
                             extra_free=None):
            toks = []
            for tc, ap in nat_tiles:
                w = [nat_tok_fn(tc)]
                if trans_tok["free"] is not None:
                    w.append(trans_tok["free"])
                for dc in range(8):
                    tk = transp(TRANS[:, dc * 128:(dc + 1) * 128],
                                ap[:, dc * 128:(dc + 1) * 128], ident,
                                inc=TE if dc == 7 else None, waits=w if dc == 0 else ())
                trans_tok["ready"] = tk
                etoks = []
                for dc in range(8):
                    etk = vop("tensor_copy",
                              dst_block_fn(dc)[:, 1 + tc * 128:1 + (tc + 1) * 128],
                              TRANS[:, dc * 128:(dc + 1) * 128],
                              inc=VE if dc == 7 else None,
                              waits=[trans_tok["ready"]] if dc == 0 else ())
                trans_tok["free"] = etk
                toks.append(etk)
            return toks

        def phaseA(s):
            sdep = []
            barrier()
            pending_xt = []

            def flush_xt_writeback():
                while pending_xt:
                    tc, sl, cb = pending_xt.pop(0)
                    w = [cb]
                    if trans_tok["free"] is not None:
                        w.append(trans_tok["free"])
                    for dc in range(8):
                        sv = bfv((A_WB if sl < 2 else A_XB2 - 2048)
                                 + sl * 1024 + dc * 128, 128)
                        tk = transp(TRANS[:, dc * 128:(dc + 1) * 128],
                                    sv, IDENT[:, :],
                                    inc=TE if dc == 7 else None,
                                    waits=w if dc == 0 else ())
                    trans_tok["ready"] = tk
                    wb_slot[sl] = tk
                    etk = None
                    for dc in range(8):
                        etk = vop("tensor_copy",
                                  xt_block(dc)[:, 1 + tc * 128:
                                               1 + (tc + 1) * 128],
                                  TRANS[:, dc * 128:(dc + 1) * 128],
                                  inc=VE if dc == 7 else None,
                                  waits=[trans_tok["ready"]] if dc == 0 else ())
                    trans_tok["free"] = etk
                    ln_sp_toks.append(etk)
            bias_tok = build_bias_bcasts(None)

            ones_ap = bfv(A_V65, 16 * 1040).rearrange(
                "p (t h e) -> p (t h) e", h=16, e=65)[:, :, 64:65]

            xb_toks = {}
            for tc in range(NTC):
                xb_toks[tc] = dma("g", bfv(A_V65 + tc * 1024, 1024),
                                  xin[s, tc * 128:(tc + 1) * 128, :], QG)
            xb_all = xb_toks[NTC - 1]
            build_T_from_nat([(tc, bfv(A_V65 + tc * 1024, 1024)) for tc in range(NTC)],
                             xt_block, IDENT[:, :], lambda tc: xb_all)
            xt_ready = trans_tok["free"]

            def stream_wt(oc_list, dst_fn, bias_col_fn, src_row0, tag):
                slot_tok = [None, None]
                for idx, oc in enumerate(oc_list):
                    sl = idx % 2
                    wb = bfv(A_WB + sl * 1024, 1024)
                    wld = dma("g", wb, di["qkvw"][src_row0 + oc * 128:
                                                  src_row0 + (oc + 1) * 128, :],
                              QW[sl], waits=[slot_tok[sl]] if slot_tok[sl] else ())
                    wb2 = bfv(A_WB + sl * 1024, 1024)
                    w = [wld]
                    if trans_tok["free"] is not None:
                        w.append(trans_tok["free"])
                    for dc in range(8):
                        tk = transp(TRANS[:, dc * 128:(dc + 1) * 128],
                                    wb[:, dc * 128:(dc + 1) * 128], IDENT[:, :],
                                    inc=TE if dc == 7 else None,
                                    waits=w if dc == 0 else ())
                    trans_tok["ready"] = tk
                    etk = None
                    for dc in range(8):
                        etk = vop("tensor_copy", wb2[:, dc * 128:(dc + 1) * 128],
                                  TRANS[:, dc * 128:(dc + 1) * 128],
                                  inc=VE if dc == 7 else None,
                                  waits=[trans_tok["ready"]] if dc == 0 else ())
                    trans_tok["free"] = etk
                    mtk = None
                    for dc in range(8):
                        for tt in range(4):
                            w2 = [etk, xt_ready] if (dc == 0 and tt == 0) else ()
                            if dc == 0 and bank[tt]["free"] is not None:
                                w2 = list(w2) + [bank[tt]["free"]]
                            mtk = mm(PS[tt][:, :], wb2[:, dc * 128:(dc + 1) * 128],
                                     xt_block(dc)[:, 1 + tt * 512:1 + (tt + 1) * 512],
                                     start=(dc == 0), stop=(dc == 7),
                                     inc=TE if (dc == 7 and tt == 3) else None,
                                     waits=w2)
                    ev = None
                    for tt in range(4):
                        ev = act(dst_fn(oc)[:, tt * 512:(tt + 1) * 512],
                                 PS[tt][:, :], AF.Identity,
                                 bias=QKVB[:, bias_col_fn(oc):bias_col_fn(oc) + 1],
                                 scale=1.0, inc=SE,
                                 waits=[mtk] if tt == 0 else ())
                        bank[tt]["ready"] = mtk
                        bank[tt]["free"] = ev
                    slot_tok[sl] = ev
                return slot_tok

            stream_wt(list(range(8)), lambda oc: bfv(A_KT + oc * 2048, 2048),
                      lambda oc: 8 + oc, D, "K")
            kt_done = bank[0]["free"]
            kt_done2 = bank[1]["free"]

            wvt_toks = []
            for oc in range(8):
                sl = oc % 2
                wb = bfv(A_WB + sl * 1024, 1024)
                wld = dma("g", wb, di["qkvw"][2 * D + oc * 128:2 * D + (oc + 1) * 128, :],
                          QW[sl], waits=[bank[0]["free"], bank[1]["free"]]
                          if oc < 2 else [wvt_toks[oc - 2]])
                w = [wld, trans_tok["free"]]
                for dc in range(8):
                    tk = transp(TRANS[:, dc * 128:(dc + 1) * 128],
                                wb[:, dc * 128:(dc + 1) * 128], IDENT[:, :],
                                inc=TE if dc == 7 else None, waits=w if dc == 0 else ())
                trans_tok["ready"] = tk
                etk = None
                for dc in range(8):
                    etk = vop("tensor_copy",
                              bfv(A_QTL + dc * 1024 + oc * 128, 128),
                              TRANS[:, dc * 128:(dc + 1) * 128],
                              inc=VE if dc == 7 else None,
                              waits=[trans_tok["ready"]] if dc == 0 else ())
                trans_tok["free"] = etk
                wvt_toks.append(etk)
            wvt_done = wvt_toks[-1]

            ones_tok = vop("memset", ones_ap, 1.0, inc=VE, waits=[xt_ready])
            v_done = []
            for tc in range(NTC):
                bb = 2 + 2 * (tc % 2)
                mtk = None
                for dc in range(8):
                    for ot in range(2):
                        w2 = [wvt_done] if (dc == 0 and ot == 0) else ()
                        if dc == 0 and bank[bb + ot]["free"] is not None:
                            w2 = list(w2) + [bank[bb + ot]["free"]]
                        mtk = mm(PS[bb + ot][:, :],
                                 xt_block(dc)[:, 1 + tc * 128:1 + (tc + 1) * 128],
                                 bfv(A_QTL + dc * 1024 + ot * 512, 512),
                                 start=(dc == 0), stop=(dc == 7),
                                 inc=TE if (dc == 7 and ot == 1) else None,
                                 waits=w2)
                ev = None
                for ot in range(2):
                    bank[bb + ot]["ready"] = mtk
                    for h in range(8):
                        gh = ot * 8 + h
                        ev = vop("tensor_add",
                                 bfv(A_V65 + tc * 1040 + gh * 65, 64),
                                 PS[bb + ot][:, h * 64:(h + 1) * 64],
                                 f32v(A_BV // 2 + ot * 512 + h * 64, 64),
                                 inc=VE if h == 7 else None,
                                 waits=[mtk, ones_tok] if (ot == 0 and h == 0)
                                 else ())
                    bank[bb + ot]["free"] = ev
                    v_done.append(ev)
            v65_ready = v_done[-1]

            wqt_toks = []
            for oc in range(8):
                sl = oc % 2
                wb = bfv(A_WB + sl * 1024, 1024)
                wld = dma("g", wb, di["qkvw"][oc * 128:(oc + 1) * 128, :],
                          QW[sl], waits=[wvt_toks[-1]] if oc < 2 else [wqt_toks[oc - 2]])
                w = [wld, trans_tok["free"]]
                for dc in range(8):
                    tk = transp(TRANS[:, dc * 128:(dc + 1) * 128],
                                wb[:, dc * 128:(dc + 1) * 128], IDENT[:, :],
                                inc=TE if dc == 7 else None, waits=w if dc == 0 else ())
                trans_tok["ready"] = tk
                etk = None
                for dc in range(8):
                    etk = vop("tensor_copy",
                              bfv(A_WQT + dc * 1024 + oc * 128, 128),
                              TRANS[:, dc * 128:(dc + 1) * 128],
                              inc=VE if dc == 7 else None,
                              waits=[trans_tok["ready"]] if dc == 0 else ())
                trans_tok["free"] = etk
                wqt_toks.append(etk)
            wqt_done = wqt_toks[-1]

            wpt_toks = []
            for ec in range(8):
                sl = ec % 2
                wb = bfv(A_WB + sl * 1024, 1024)
                wld = dma("g", wb, di["projw"][ec * 128:(ec + 1) * 128, :],
                          QW[sl], waits=[wqt_toks[-1]] if ec < 2 else [wpt_toks[ec - 2]])
                w = [wld, trans_tok["free"]]
                for oc in range(8):
                    tk = transp(TRANS[:, oc * 128:(oc + 1) * 128],
                                wb[:, oc * 128:(oc + 1) * 128], IDENT[:, :],
                                inc=TE if oc == 7 else None, waits=w if oc == 0 else ())
                trans_tok["ready"] = tk
                etk = None
                for oc in range(8):
                    etk = vop("tensor_copy",
                              bfv(A_WPT + oc * 1024 + ec * 128, 128),
                              TRANS[:, oc * 128:(oc + 1) * 128],
                              inc=VE if oc == 7 else None,
                              waits=[trans_tok["ready"]] if oc == 0 else ())
                trans_tok["free"] = etk
                wpt_toks.append(etk)
            wpt_done = wpt_toks[-1]

            if "XT" in dump and s == 0:
                for dc in range(8):
                    sdep.append(dma("y", dbg["XT"][dc * 128:(dc + 1) * 128, :],
                                    xt_block(dc), QY, waits=[xt_ready]))
            if "KT" in dump and s == 0:
                for oc in range(8):
                    sdep.append(dma("y", dbg["KT"][oc * 128:(oc + 1) * 128, :],
                                    bfv(A_KT + oc * 2048, 2048), QY,
                                    waits=[kt_done, kt_done2]))
            if "V65" in dump and s == 0:
                for tc in range(NTC):
                    sdep.append(dma("y", dbg["V65"][tc * 128:(tc + 1) * 128, :],
                                    bfv(A_V65 + tc * 1040, 1040), QY,
                                    waits=[v65_ready]))

            ln_sp_toks = []
            xw_free = [None] * 4
            wb_slot = [None, None, None, None]
            for qt in range(4):
                for tc4 in range(4):
                    tc = qt * 4 + tc4
                    xw_all = dma("g", f32v(A_XW // 2 + tc4 * 1024, 1024),
                                 xin[s, tc * 128:(tc + 1) * 128, :], QX,
                                 waits=[xw_free[tc4]] if xw_free[tc4] else ())
                for oc in range(8):
                    b = oc % 2
                    mtk = None
                    for dc in range(8):
                        w2 = [wqt_done, xt_ready] if dc == 0 else ()
                        if dc == 0 and bank[b]["free"] is not None:
                            w2 = list(w2) + [bank[b]["free"]]
                        mtk = mm(PS[b][:, :], bfv(A_WQT + dc * 1024 + oc * 128, 128),
                                 xt_block(dc)[:, 1 + qt * 512:1 + (qt + 1) * 512],
                                 start=(dc == 0), stop=(dc == 7),
                                 inc=TE if dc == 7 else None, waits=w2)
                    bank[b]["ready"] = mtk
                    ev = act(bfv(A_QTL + oc * 512, 512), PS[b][:, :], AF.Identity,
                             bias=QKVB[:, oc:oc + 1], scale=1.0, inc=SE,
                             waits=[bank[b]["ready"]])
                    bank[b]["free"] = ev
                qtl_ready = bank[0]["free"]
                qtl_ready2 = bank[1]["free"]

                if "QT" in dump and s == 0 and qt == 0:
                    for oc in range(8):
                        sdep.append(dma("y", dbg["QT"][oc * 128:(oc + 1) * 128, :],
                                        bfv(A_QTL + oc * 512, 512), QY,
                                        waits=[qtl_ready, qtl_ready2]))

                TRANSF = TRANS.bitcast(F32)
                pctr = [0]
                pslot_tok = [None] * 6
                for pg in range(4):
                    prs = (2 * pg, 2 * pg + 1)
                    sbank = {prs[0]: (0, 1), prs[1]: (2, 3)}
                    obank = {prs[0]: (PS[4], PS[5]), prs[1]: (PS[6], TRANSF)}
                    exp_prev = {p: [None, None] for p in prs}
                    exp_cur = {p: [None, None] for p in prs}
                    avts = {p: [None, None] for p in prs}
                    for kc in range(16):
                        for p in prs:
                            b0, b1 = sbank[p]
                            wS = [qtl_ready, qtl_ready2, kt_done, kt_done2]
                            if bank[b0]["free"] is not None:
                                wS.append(bank[b0]["free"])
                            mm(PS[b0][:, :],
                               bfv(A_KT + p * 2048 + kc * 128, 128)[0:64, :],
                               bfv(A_QTL + p * 512, 512)[0:64, :],
                               start=True, stop=True, waits=wS)
                            wS2 = ([bank[b1]["free"]]
                                   if bank[b1]["free"] is not None else ())
                            t2 = mm(PS[b1][:, :],
                                    bfv(A_KT + p * 2048 + kc * 128, 128)[64:128, :],
                                    bfv(A_QTL + p * 512, 512)[64:128, :],
                                    start=True, stop=True, inc=TE, waits=wS2)
                            bank[b0]["ready"] = t2
                            bank[b1]["ready"] = t2
                            for hh, bx in ((0, b0), (1, b1)):
                                slot = pctr[0] % 6
                                pctr[0] += 1
                                wE = [t2]
                                if pslot_tok[slot] is not None:
                                    wE.append(pslot_tok[slot])
                                ek = act(bfv(A_PR + slot * 512, 512),
                                         PS[bx][:, :], AF.Exp, scale=0.125,
                                         inc=SE, waits=wE)
                                exp_cur[p][hh] = (ek, slot)
                                bank[bx]["free"] = ek
                            if kc > 0:
                                for hh in range(2):
                                    ekp, slot = exp_prev[p][hh]
                                    ob = obank[p][hh]
                                    wA = [ekp]
                                    if kc == 1:
                                        wA.append(v65_ready)
                                        if p == prs[1]:
                                            fr = (bank[6]["free"] if hh == 0
                                                  else trans_tok["free"])
                                        else:
                                            fr = bank[4 + hh]["free"]
                                        if fr is not None:
                                            wA.append(fr)
                                    avts[p][hh] = mm(
                                        ob[0:65, :],
                                        bfv(A_V65 + (kc - 1) * 1040 +
                                            (p * 2 + hh) * 65, 65),
                                        bfv(A_PR + slot * 512, 512),
                                        start=(kc == 1), stop=False,
                                        inc=TE, waits=wA)
                                    pslot_tok[slot] = avts[p][hh]
                            exp_prev[p] = exp_cur[p]
                            exp_cur[p] = [None, None]
                    for p in prs:
                        for hh in range(2):
                            ekp, slot = exp_prev[p][hh]
                            avts[p][hh] = mm(obank[p][hh][0:65, :],
                                             bfv(A_V65 + 15 * 1040 +
                                                 (p * 2 + hh) * 65, 65),
                                             bfv(A_PR + slot * 512, 512),
                                             start=False, stop=True,
                                             inc=TE, waits=[ekp])
                            pslot_tok[slot] = avts[p][hh]
                    for p in prs:
                        o0, o1 = obank[p]
                        rbw = ARF[:, A_RB // 2:A_RB // 2 + 512]
                        vop("tensor_copy", rbw[0:1, :], o0[64:65, :],
                            waits=[avts[p][0], sq_fence[0]])
                        rc = vop("tensor_copy", rbw[32:33, :], o1[64:65, :],
                                 inc=VE, waits=[avts[p][1]])
                        act(rbw[0:33, :], rbw[0:33, :], AF.Ln, waits=[rc])
                        rc2 = act(rbw[0:33, :], rbw[0:33, :], AF.Exp,
                                  scale=-1.0, inc=SE)
                        sh = None
                        for b32 in range(4):
                            srow = 0 if b32 < 2 else 32
                            sh = vop("stream_shuffle",
                                     f32v(A_SQ // 2, 512)
                                     [b32 * 32:(b32 + 1) * 32, :],
                                     rbw[srow:srow + 32, :], [0] * 32,
                                     inc=VE if b32 == 3 else None,
                                     waits=[rc2] if b32 == 0 else ())
                        vop("tensor_mul", bfv(A_OQ + p * 512, 512)[0:64, :],
                            o0[0:64, :], f32v(A_SQ // 2, 512)[0:64, :],
                            waits=[sh])
                        m2 = vop("tensor_mul",
                                 bfv(A_OQ + p * 512, 512)[64:128, :],
                                 o1[0:64, :], f32v(A_SQ // 2, 512)[64:128, :],
                                 inc=VE)
                        sq_fence[0] = m2
                        if p == prs[0]:
                            bank[4]["free"] = m2
                            bank[5]["free"] = m2
                        else:
                            bank[6]["free"] = m2
                            trans_tok["free"] = m2
                oq_ready = m2
                flush_xt_writeback()

                if "OQ" in dump and s == 0 and qt == 0:
                    for oc in range(8):
                        sdep.append(dma("y", dbg["OQ"][oc * 128:(oc + 1) * 128, :],
                                        bfv(A_OQ + oc * 512, 512), QY,
                                        waits=[oq_ready]))

                for tc4 in range(4):
                    tc = qt * 4 + tc4
                    b0 = (tc4 % 2) * 2
                    mtk = None
                    for oc in range(8):
                        for et in range(2):
                            w2 = [oq_ready] if (oc == 0 and et == 0) else ()
                            if oc == 0 and bank[b0 + et]["free"] is not None:
                                w2 = list(w2) + [bank[b0 + et]["free"]]
                            mtk = mm(PS[b0 + et][:, :],
                                     bfv(A_OQ + oc * 512 + tc4 * 128, 128),
                                     bfv(A_WPT + oc * 1024 + et * 512, 512),
                                     start=(oc == 0), stop=(oc == 7),
                                     inc=TE if (oc == 7 and et == 1) else None,
                                     waits=w2)
                    a2 = None
                    for et in range(2):
                        b = b0 + et
                        bank[b]["ready"] = mtk
                        xw = f32v(A_XW // 2 + tc4 * 1024 + et * 512, 512)
                        vop("tensor_add", xw, PS[b][:, :], xw,
                            waits=[mtk, xw_all] if et == 0 else ())
                        a2 = vop("tensor_add", xw, xw,
                                 f32v(A_PB // 2 + et * 512, 512), inc=VE)
                        bank[b]["free"] = a2
                    if "XWIN" in dump and s == 0 and qt == 0:
                        sdep.append(dma("y", dbg["XWIN"][tc4 * 128:(tc4 + 1) * 128, :],
                                        f32v(A_XW // 2 + tc4 * 1024, 1024), QY,
                                        waits=[a2]))

                xws = [f32v(A_XW // 2 + t4 * 1024, 1024) for t4 in range(4)]
                SUMc = [STAT[:, 4 * t4:4 * t4 + 1] for t4 in range(4)]
                VARc = [STAT[:, 16 + 4 * t4:17 + 4 * t4] for t4 in range(4)]
                M2c = [STAT[:, 32 + 4 * t4:33 + 4 * t4] for t4 in range(4)]
                r1 = None
                for t4 in range(4):
                    r1 = vop("reduce_sum", SUMc[t4], xws[t4], AX.X,
                             inc=VE if t4 == 3 else None,
                             waits=[a2] if t4 == 0 else ())
                for t4 in range(4):
                    vop("tensor_mul", f32v(A_SQ // 2, 1024), xws[t4], xws[t4],
                        waits=[sq_fence[0]] if t4 == 0 else ())
                    vop("reduce_sum", VARc[t4], f32v(A_SQ // 2, 1024), AX.X)
                for t4 in range(4):
                    vop("tensor_scalar", M2c[t4], SUMc[t4], SUMc[t4],
                        EPS_LN * D * D, ALU.mult, ALU.subtract)
                for t4 in range(4):
                    vop("tensor_scalar_mul", M2c[t4], M2c[t4], 1.0 / (D * D))
                for t4 in range(4):
                    vop("tensor_scalar_mul", SUMc[t4], SUMc[t4], 1.0 / D)
                for t4 in range(4):
                    vop("tensor_scalar_mul", VARc[t4], VARc[t4], 1.0 / D)
                v3 = None
                for t4 in range(4):
                    v3 = vop("tensor_sub", VARc[t4], VARc[t4], M2c[t4],
                             inc=VE if t4 == 3 else None)
                sq_fence[0] = v3
                for t4 in range(4):
                    act(VARc[t4], VARc[t4], AF.Ln,
                        waits=[v3] if t4 == 0 else ())
                l2 = None
                for t4 in range(4):
                    l2 = act(VARc[t4], VARc[t4], AF.Exp, scale=-0.5,
                             inc=SE if t4 == 3 else None)
                if "STT2" in dump and s == 0 and qt == 0:
                    sdep.append(dma("y", dbg["STT2"][:, 0:48],
                                    STAT[:, 0:48], QY, waits=[l2]))
                vop("tensor_copy", f32v(A_SQ // 2, 512), xws[0][:, 0:512],
                    waits=[l2])
                n1 = None
                for t4 in range(4):
                    n1 = vop("tensor_scalar", xws[t4], xws[t4], SUMc[t4],
                             VARc[t4], ALU.subtract, ALU.mult,
                             inc=VE if t4 == 3 else None)
                for t4 in range(4):
                    tc = qt * 4 + t4
                    sl = t4
                    wcb = [n1] if t4 == 0 else []
                    if wb_slot[sl] is not None:
                        wcb.append(wb_slot[sl])
                    dst = bfv((A_WB if sl < 2 else A_XB2 - 2048)
                              + sl * 1024, 1024)
                    cb = act(dst, xws[t4], AF.Copy,
                             inc=SE, waits=wcb)
                    sp = dma("g", xpa[tc * 128:(tc + 1) * 128, :], xws[t4], QG,
                             waits=[cb])
                    ln_sp_toks.append(sp)
                    xw_free[t4] = sp
                    pending_xt.append((tc, sl, cb))
                if qt == 3:
                    flush_xt_writeback()
            return ln_sp_toks, sdep

        allsems = [TE, VE, SE, QG, QY, QX] + QW

        def barrier():
            snap = [(s_, s_.n) for s_ in allsems if s_.n > 0]
            for e in Prog.ENGS:
                for tk in snap:
                    P.wait(e, tk)

        def x_chunk(tc):
            return f32v(tc * 1024, 1024)

        def xt_shift_cols(dc, c0, n):
            off = 1 if dc < 4 else 0
            return xt_block(dc)[:, off + c0: off + c0 + n]

        def rebuild_xt(s, entry=None):
            wb_slot = [None, None]
            for tc in range(NTC):
                sl = tc % 2
                wcb = [wb_slot[sl]] if wb_slot[sl] else []
                if tc == 0 and entry is not None:
                    wcb.append(entry)
                cb = act(bfv(B_XBN + sl * 1024, 1024), x_chunk(tc), AF.Copy,
                         inc=SE, waits=wcb)
                w = [cb]
                if trans_tok["free"] is not None:
                    w.append(trans_tok["free"])
                for dc in range(8):
                    tk = transp(TRANS[:, dc * 128:(dc + 1) * 128],
                                bfv(B_XBN + sl * 1024 + dc * 128, 128),
                                IDENT[:, :], inc=TE if dc == 7 else None,
                                waits=w if dc == 0 else ())
                trans_tok["ready"] = tk
                wb_slot[sl] = tk
                etk = None
                for dc in range(8):
                    etk = vop("tensor_copy",
                              xt_block(dc)[:, 1 + tc * 128:1 + (tc + 1) * 128],
                              TRANS[:, dc * 128:(dc + 1) * 128],
                              inc=VE if dc == 7 else None,
                              waits=[trans_tok["ready"]] if dc == 0 else ())
                trans_tok["free"] = etk
            return etk

        def phaseB(s, l):
            sdep = []
            barrier()
            if l == 0:
                xl = None
                for tc in range(NTC):
                    xl = dma("y", x_chunk(tc), xpa[tc * 128:(tc + 1) * 128, :],
                             QY)
                xload = xl
            else:
                xload = rebuild_xt(s)

            ones_ap = bfv(B_V65, 16 * 520).rearrange(
                "p (t h e) -> p (t h) e", h=8, e=65)[:, :, 64:65]
            ones_tok = vop("memset", ones_ap, 1.0, waits=[xload])
            wvb = bfv(B_PR, 8 * 512)
            wvld = dma("g", wvb.rearrange("p (c m) -> p c m", c=8),
                       di["fwqkv"][l].rearrange("(c p) m -> p c m", p=128)
                       [:, :, 1024:1536], QW[2], waits=[xload])

            def v2_chunk(tc):
                b = 4 + tc % 2
                mtk = None
                for dc in range(8):
                    w2 = [wvld, xload] if dc == 0 else ()
                    if dc == 0 and bank[b]["free"] is not None:
                        w2 = list(w2) + [bank[b]["free"]]
                    mtk = mm(PS[b][:, :], xt_shift_cols(dc, tc * 128, 128),
                             wvb[:, dc * 512:(dc + 1) * 512],
                             start=(dc == 0), stop=(dc == 7),
                             inc=TE if dc == 7 else None, waits=w2)
                bank[b]["ready"] = mtk
                ev = None
                for h in range(8):
                    ev = vop("tensor_copy",
                             bfv(B_V65 + tc * 520 + h * 65, 64),
                             PS[b][:, h * 64:(h + 1) * 64],
                             inc=VE if h == 7 else None,
                             waits=[mtk, ones_tok] if h == 0 else ())
                bank[b]["free"] = ev
                return ev

            q2r = f32v(B_Q2R // 2, 2048)
            sqb = bfv(B_SQB, 2048)
            rbrow = ARF[:, B_RB // 2:B_RB // 2 + 512]
            rbc = f32v(B_RBC // 2, 2048)
            slot_tok = [None, None]
            qk_tok = None
            for oc in range(8):
                sl = oc % 2
                wb = bfv(B_WB + sl * 1024, 1024)
                wld = dma("g", wb.rearrange("p (c m) -> p c m", c=8),
                          di["fwqkv"][l].rearrange("(c p) m -> p c m", p=128)
                          [:, :, oc * 128:(oc + 1) * 128], QW[sl],
                          waits=[slot_tok[sl], xload] if slot_tok[sl]
                          else [xload])
                mtk = None
                for dc in range(8):
                    for tt in range(4):
                        w2 = [wld] if (dc == 0 and tt == 0) else ()
                        if dc == 0 and bank[tt]["free"] is not None:
                            w2 = list(w2) + [bank[tt]["free"]]
                        mtk = mm(PS[tt][:, :], wb[:, dc * 128:(dc + 1) * 128],
                                 xt_shift_cols(dc, tt * 512, 512),
                                 start=(dc == 0), stop=(dc == 7),
                                 inc=TE if (dc == 7 and tt == 3) else None,
                                 waits=w2)
                v_done = v2_chunk(2 * oc)
                v_done = v2_chunk(2 * oc + 1)
                ev = None
                for tt in range(4):
                    bank[tt]["ready"] = mtk
                    wev = [mtk] if tt == 0 else []
                    if tt == 0 and qk_tok is not None:
                        wev.append(qk_tok)
                    ev = act(q2r[:, tt * 512:(tt + 1) * 512], PS[tt][:, :],
                             AF.Identity, scale=1.0, inc=SE, waits=wev)
                    bank[tt]["free"] = ev
                slot_tok[sl] = ev
                if "Q2R" in dump and s == 0 and l == 0 and oc == 0:
                    sdep.append(dma("y", dbg["Q2R"], q2r, QY, waits=[ev]))
                    barrier()
                sq1 = vop("tensor_mul", sqb, q2r, q2r, inc=VE, waits=[ev])
                for tt in range(4):
                    wS = [sq1]
                    if bank[6]["free"] is not None:
                        wS.append(bank[6]["free"])
                    smk = mm(PS[6][0:2, :], ONES2[:, :],
                             sqb[:, tt * 512:(tt + 1) * 512],
                             start=True, stop=True, inc=TE, waits=wS)
                    bank[6]["ready"] = smk
                    ec = vop("tensor_copy", rbrow[0:2, :], PS[6][0:2, :],
                             inc=VE, waits=[smk])
                    bank[6]["free"] = ec
                    c1 = vop("tensor_scalar_max", rbrow[0:2, :], rbrow[0:2, :],
                             1e-24, inc=VE)
                    act(rbrow[0:2, :], rbrow[0:2, :], AF.Ln, waits=[c1])
                    a4 = act(rbrow[0:2, :], rbrow[0:2, :], AF.Exp,
                             scale=-0.5, inc=SE)
                    sh = None
                    for b32 in range(4):
                        row = 0 if b32 < 2 else 1
                        sh = vop("stream_shuffle",
                                 rbc[b32 * 32:(b32 + 1) * 32,
                                     tt * 512:(tt + 1) * 512],
                                 rbrow[0:32, :], [row] * 32,
                                 inc=VE if b32 == 3 else None,
                                 waits=[a4] if b32 == 0 else ())
                if "RBC" in dump and s == 0 and l == 0 and oc == 0:
                    sdep.append(dma("y", dbg["RBC"], rbc, QY, waits=[sh]))
                    barrier()
                isq = oc < 4
                qs_col = (QS8 if isq else KSC)[:, l:l + 1]
                cs_t = CSQ2 if isq else CSK2
                sn_t = SNQ2 if isq else SNK2
                dstT = bfv((B_QT2 + oc * 2048) if isq
                           else (B_KT2 + (oc - 4) * 2048), 2048)
                vop("scalar_tensor_tensor", q2r, q2r, qs_col, rbc,
                    ALU.mult, ALU.mult, waits=[sh])
                vop("tensor_copy", sqb[0:32, :], q2r[32:64, :])
                vop("tensor_copy", sqb[32:64, :], q2r[0:32, :])
                vop("tensor_copy", sqb[64:96, :], q2r[96:128, :])
                vop("tensor_copy", sqb[96:128, :], q2r[64:96, :])
                vop("tensor_mul", q2r, q2r, cs_t)
                vop("tensor_mul", sqb, sqb, sn_t)
                qk_tok = vop("tensor_add", dstT, q2r, sqb, inc=VE)
            qkv2_done = qk_tok

            if "QT2" in dump and s == 0 and l == 0:
                for oc in range(4):
                    sdep.append(dma("y", dbg["QT2"][oc * 128:(oc + 1) * 128, :],
                                    bfv(B_QT2 + oc * 2048, 2048), QY,
                                    waits=[qkv2_done]))
            if dump:
                barrier()
            if "KT2" in dump and s == 0 and l == 0:
                for oc in range(4):
                    sdep.append(dma("y", dbg["KT2"][oc * 128:(oc + 1) * 128, :],
                                    bfv(B_KT2 + oc * 2048, 2048), QY,
                                    waits=[qkv2_done]))
            if dump:
                barrier()

            if "V65L" in dump and s == 0 and l == 0:
                for tc in range(NTC):
                    sdep.append(dma("y", dbg["V65L"][tc * 128:(tc + 1) * 128, :],
                                    bfv(B_V65 + tc * 520, 520), QY,
                                    waits=[v_done]))
            if dump:
                barrier()

            wob = bfv(B_WB, 4 * 1024)
            wold = dma("g", wob.rearrange("p (c m) -> p c m", c=4),
                       di["fwo"][l].rearrange("(c p) m -> p c m", p=128),
                       QW[3], waits=[v_done])
            obc = f32v(B_RBC // 2, 512)
            wo_tok = None
            TRANSF = TRANS.bitcast(F32)
            for w_ in range(4):
                chunks = []
                if w_ > 0:
                    for j in range(4):
                        chunks.append(("U", (w_ - 1) * 4 + j, 0,
                                       (j + 1) * 128, j * 128))
                for j in range(4):
                    chunks.append(("C", w_ * 4 + j, j * 128,
                                   512 - j * 128, j * 128))
                nch = len(chunks)
                pslot_tok = [None] * 6
                pctr = [0]
                m2_ = None
                for pgl in range(2):
                    prs = (2 * pgl, 2 * pgl + 1)
                    sbank = {prs[0]: (0, 1), prs[1]: (2, 3)}
                    obank = {prs[0]: (PS[4], PS[5]), prs[1]: (PS[6], TRANSF)}
                    exp_prev = {p: [None, None] for p in prs}
                    avts = {p: [None, None] for p in prs}
                    meta_prev = {p: None for p in prs}
                    for ci, (kind, kg, qoff, nal, moff) in enumerate(chunks):
                        for p in prs:
                            b0, b1 = sbank[p]
                            wS = [qkv2_done]
                            if wo_tok is not None:
                                wS.append(wo_tok)
                            if bank[b0]["free"] is not None:
                                wS.append(bank[b0]["free"])
                            kslice = bfv(B_KT2 + p * 2048 + kg * 128, 128)
                            qslice = bfv(B_QT2 + p * 2048 + w_ * 512 + qoff,
                                         nal)
                            mm(PS[b0][:, 0:nal], kslice[0:64, :],
                               qslice[0:64, :], start=True, stop=True,
                               waits=wS)
                            wS2 = ([bank[b1]["free"]]
                                   if bank[b1]["free"] is not None else ())
                            t2 = mm(PS[b1][:, 0:nal], kslice[64:128, :],
                                    qslice[64:128, :], start=True, stop=True,
                                    inc=TE, waits=wS2)
                            bank[b0]["ready"] = t2
                            bank[b1]["ready"] = t2
                            msk = MASKC if kind == "C" else MASKU
                            vop("tensor_add",
                                PS[b0][:, moff - qoff:moff - qoff + 128],
                                PS[b0][:, moff - qoff:moff - qoff + 128],
                                msk[:, :], waits=[t2])
                            mk2 = vop("tensor_add",
                                      PS[b1][:, moff - qoff:moff - qoff + 128],
                                      PS[b1][:, moff - qoff:moff - qoff + 128],
                                      msk[:, :], inc=VE)
                            for hh, bx in ((0, b0), (1, b1)):
                                slot = pctr[0] % 6
                                pctr[0] += 1
                                wE = [mk2]
                                if pslot_tok[slot] is not None:
                                    wE.append(pslot_tok[slot])
                                ek = act(bfv(B_PR + slot * 512, nal),
                                         PS[bx][:, 0:nal], AF.Exp, inc=SE,
                                         waits=wE)
                                bank[bx]["free"] = ek
                                exp_cur = (ek, slot)
                                if hh == 0:
                                    exp_c0 = exp_cur
                                else:
                                    exp_c1 = exp_cur
                            if ci > 0:
                                pkind, pkg, pqoff, pnal, pmoff = meta_prev[p]
                                for hh in range(2):
                                    ekp, slot = exp_prev[p][hh]
                                    ob = obank[p][hh]
                                    wA = [ekp]
                                    if ci == 1:
                                        wA.append(v_done)
                                        if p == prs[1]:
                                            fr = (bank[6]["free"] if hh == 0
                                                  else trans_tok["free"])
                                        else:
                                            fr = bank[4 + hh]["free"]
                                        if fr is not None:
                                            wA.append(fr)
                                    avts[p][hh] = mm(
                                        ob[0:65, pqoff:pqoff + pnal],
                                        bfv(B_V65 + pkg * 520 +
                                            (p * 2 + hh) * 65, 65),
                                        bfv(B_PR + slot * 512, pnal),
                                        start=(ci == 1), stop=False,
                                        inc=TE, waits=wA)
                                    pslot_tok[slot] = avts[p][hh]
                            exp_prev[p] = [exp_c0, exp_c1]
                            meta_prev[p] = (kind, kg, qoff, nal, moff)
                    for p in prs:
                        pkind, pkg, pqoff, pnal, pmoff = meta_prev[p]
                        for hh in range(2):
                            ekp, slot = exp_prev[p][hh]
                            avts[p][hh] = mm(
                                obank[p][hh][0:65, pqoff:pqoff + pnal],
                                bfv(B_V65 + pkg * 520 + (p * 2 + hh) * 65, 65),
                                bfv(B_PR + slot * 512, pnal),
                                start=False, stop=True, inc=TE, waits=[ekp])
                            pslot_tok[slot] = avts[p][hh]
                    for p in prs:
                        o0, o1 = obank[p]
                        vop("tensor_copy", rbrow[0:1, :], o0[64:65, :],
                            waits=[avts[p][0]])
                        rc = vop("tensor_copy", rbrow[32:33, :], o1[64:65, :],
                                 inc=VE, waits=[avts[p][1]])
                        act(rbrow[0:33, :], rbrow[0:33, :], AF.Ln, waits=[rc])
                        rc2 = act(rbrow[0:33, :], rbrow[0:33, :], AF.Exp,
                                  scale=-1.0, inc=SE)
                        sh = None
                        for b32 in range(4):
                            srow = 0 if b32 < 2 else 32
                            sh = vop("stream_shuffle",
                                     obc[b32 * 32:(b32 + 1) * 32, :],
                                     rbrow[srow:srow + 32, :], [0] * 32,
                                     inc=VE if b32 == 3 else None,
                                     waits=[rc2] if b32 == 0 else ())
                        vop("tensor_mul", bfv(B_OQL + p * 512, 512)[0:64, :],
                            o0[0:64, :], obc[0:64, :], waits=[sh])
                        m2_ = vop("tensor_mul",
                                  bfv(B_OQL + p * 512, 512)[64:128, :],
                                  o1[0:64, :], obc[64:128, :], inc=VE)
                        if p == prs[0]:
                            bank[4]["free"] = m2_
                            bank[5]["free"] = m2_
                        else:
                            bank[6]["free"] = m2_
                            trans_tok["free"] = m2_
                oql_ready = m2_
                if "OQL" in dump and s == 0 and l == 0 and w_ == 1:
                    for oc in range(4):
                        sdep.append(dma("y",
                                        dbg["OQL"][oc * 128:(oc + 1) * 128, :],
                                        bfv(B_OQL + oc * 512, 512), QY,
                                        waits=[oql_ready]))
                for tc4 in range(4):
                    tc = w_ * 4 + tc4
                    b0 = (tc4 % 2) * 2
                    mtk = None
                    for oc in range(4):
                        for et in range(2):
                            w2 = [oql_ready, wold] if (oc == 0 and et == 0) else ()
                            if oc == 0 and bank[b0 + et]["free"] is not None:
                                w2 = list(w2) + [bank[b0 + et]["free"]]
                            mtk = mm(PS[b0 + et][:, :],
                                     bfv(B_OQL + oc * 512 + tc4 * 128, 128),
                                     bfv(B_WB + oc * 1024 + et * 512, 512),
                                     start=(oc == 0), stop=(oc == 3),
                                     inc=TE if (oc == 3 and et == 1) else None,
                                     waits=w2)
                    wo_tok = mtk
                    a1 = None
                    for et in range(2):
                        b = b0 + et
                        bank[b]["ready"] = mtk
                        a1 = vop("tensor_add",
                                 x_chunk(tc)[:, et * 512:(et + 1) * 512],
                                 PS[b][:, :],
                                 x_chunk(tc)[:, et * 512:(et + 1) * 512],
                                 inc=VE,
                                 waits=[mtk, xload] if et == 0 else ())
                        bank[b]["free"] = a1
            if dump:
                barrier()
            wo_res_done = bank[(3 * 2 + 1) % 4]["free"]

            if "XB" in dump and s == 0 and l == 0:
                for tc in range(NTC):
                    sdep.append(dma("y", dbg["XB"][tc * 128:(tc + 1) * 128, :],
                                    x_chunk(tc), QY, waits=[wo_res_done]))
            if dump:
                barrier()

            w2toks = []
            for ic in range(NJ):
                na = 128 if ic < NJ - 1 else TAILJ
                w2toks.append(dma("g", bfv(M_W2 + ic * 1024, 1024)[0:na, :],
                                  di["fw2"][l, ic * 128:ic * 128 + na, :],
                                  QW[2], waits=[wo_tok] if ic == 0 else ()))
            w2ld = w2toks[-1]
            b2b = f32v(M_B2B // 2, 1024)
            last = wo_tok
            for half in range(2):
                tk1 = dma("y", ARF[0:1, B_RB // 2:B_RB // 2 + 512],
                          di["fb2"][l, half * 512:(half + 1) * 512]
                          .rearrange("(one x) -> one x", one=1), QY,
                          waits=[last])
                for b32 in range(4):
                    last = vop("stream_shuffle",
                               b2b[b32 * 32:(b32 + 1) * 32,
                                   half * 512:(half + 1) * 512],
                               ARF[0:32, B_RB // 2:B_RB // 2 + 512], [0] * 32,
                               inc=VE, waits=[tk1])
            b2b_tok = last

            xt2_tok = rebuild_xt(s, entry=wo_res_done)
            if "XTR" in dump and s == 0 and l == 0:
                for dc in range(8):
                    sdep.append(dma("y", dbg["XTR"][dc * 128:(dc + 1) * 128, :],
                                    xt_block(dc), QY, waits=[xt2_tok]))
                barrier()
            mrow = f32v(L_MBC // 2, 2048)
            rrow = f32v(L_RBC // 2, 2048)
            trow = f32v(L_TMP // 2, 2048)
            sqt8 = ([bfv(L_SQT + i * 512, 512) for i in range(6)] +
                    [bfv(L_SQT2 + i * 512, 512) for i in range(2)])
            sq_tok = [None]
            ev_ = None
            for tt in range(4):
                mtk = None
                for dc in range(8):
                    w0 = [xt2_tok] if (tt == 0 and dc == 0) else ()
                    if dc == 0 and bank[6]["free"] is not None:
                        w0 = list(w0) + [bank[6]["free"]]
                    mtk = mm(PS[6][0:1, :], ONES1[:, :],
                             xt_shift_cols(dc, tt * 512, 512),
                             start=(dc == 0), stop=(dc == 7),
                             inc=TE if dc == 7 else None, waits=w0)
                bank[6]["ready"] = mtk
                ev_ = vop("tensor_copy", mrow[0:1, tt * 512:(tt + 1) * 512],
                          PS[6][0:1, :], inc=VE, waits=[mtk])
                bank[6]["free"] = ev_
                stq = None
                for dc in range(8):
                    wsq = [ev_] if dc == 0 else ()
                    if sq_tok[0] is not None and dc == 0:
                        wsq = list(wsq) + [sq_tok[0]]
                    stq = vop("tensor_mul", sqt8[dc],
                              xt_shift_cols(dc, tt * 512, 512),
                              xt_shift_cols(dc, tt * 512, 512),
                              inc=VE if dc == 7 else None, waits=wsq)
                mtk = None
                for dc in range(8):
                    mtk = mm(PS[6][0:1, :], ONES1[:, :], sqt8[dc],
                             start=(dc == 0), stop=(dc == 7),
                             inc=TE if dc == 7 else None,
                             waits=[stq] if dc == 0 else ())
                sq_tok[0] = mtk
                bank[6]["ready"] = mtk
                ev_ = vop("tensor_copy", rrow[0:1, tt * 512:(tt + 1) * 512],
                          PS[6][0:1, :], inc=VE, waits=[mtk])
                bank[6]["free"] = ev_
            d6 = ev_
            if "LROWS" in dump and s == 0 and l == 0:
                sdep.append(dma("y", dbg["LROWS"][0:1, :], mrow[0:1, :], QY,
                                waits=[d6]))
                sdep.append(dma("y", dbg["LROWS"][1:2, :], rrow[0:1, :], QY,
                                waits=[d6]))
                barrier()
            vop("tensor_scalar_mul", mrow[0:1, :], mrow[0:1, :], 1.0 / D,
                waits=[d6])
            vop("tensor_scalar_mul", rrow[0:1, :], rrow[0:1, :], 1.0 / D)
            vop("tensor_mul", trow[0:1, :], mrow[0:1, :], mrow[0:1, :])
            vop("tensor_sub", rrow[0:1, :], rrow[0:1, :], trow[0:1, :])
            rv = vop("tensor_scalar_add", rrow[0:1, :], rrow[0:1, :], EPS_LN,
                     inc=VE)
            act(rrow[0:1, :], rrow[0:1, :], AF.Ln, waits=[rv])
            re = act(rrow[0:1, :], rrow[0:1, :], AF.Exp, scale=-0.5, inc=SE)
            sh = None
            for b32 in range(4):
                sh = vop("stream_shuffle",
                         mrow[b32 * 32:(b32 + 1) * 32, :], mrow[0:32, :],
                         [0] * 32, waits=[re] if b32 == 0 else ())
            for b32 in range(4):
                sh = vop("stream_shuffle",
                         rrow[b32 * 32:(b32 + 1) * 32, :], rrow[0:32, :],
                         [0] * 32, inc=VE if b32 == 3 else None)
            if "LMU" in dump and s == 0 and l == 0:
                sdep.append(dma("y", dbg["LMU"], mrow, QY, waits=[sh]))
                sdep.append(dma("y", dbg["LRS"], rrow, QY, waits=[sh]))
                barrier()
            trow2 = f32v(B_XBN // 2, 2048)
            hn_toks = [None] * 8
            for dc in range(8):
                tb = trow if dc % 2 == 0 else trow2
                wt1 = [sh] if dc == 0 else []
                if dc >= 2:
                    wt1.append(hn_toks[dc - 2])
                t1 = vop("tensor_sub", tb, xt_shift_cols(dc, 0, 2048), mrow,
                         waits=wt1)
                t2 = vop("tensor_mul", tb, tb, rrow, inc=VE)
                hn_toks[dc] = act(xt_block(dc)[:, 1:2049], tb, AF.Identity,
                                  scale=LNG[:, l * 8 + dc:l * 8 + dc + 1],
                                  bias=LNB[:, l * 8 + dc:l * 8 + dc + 1],
                                  inc=SE, waits=[t2])
            hn_tok = hn_toks[7]
            if "HN" in dump and s == 0 and l == 0:
                for dc in range(8):
                    sdep.append(dma("y", dbg["HN"][dc * 128:(dc + 1) * 128, :],
                                    xt_block(dc), QY, waits=[hn_tok]))
            if dump:
                barrier()

            gsc = f32v(M_GSC // 2, 512)
            wag_tok = [None, None]
            wag_base = [M_WAG, 74112]
            gel_tok = None
            for q4 in range(4):
                mj_toks = []
                for jp in range(NJ // 2):
                    j0 = 2 * jp
                    m_ = 256 if jp < NJ // 2 - 1 else 128 + TAILJ
                    sl = jp % 2
                    base = wag_base[sl]
                    wag = bfv(base, 2 * 8 * m_)
                    wwl = [hn_tok]
                    if wag_tok[sl] is not None:
                        wwl.append(wag_tok[sl])
                    wld = dma("g",
                              wag.rearrange(
                                  "p (two c m) -> p two c m", two=2, c=8),
                              di["fw1"][l].rearrange(
                                  "(c p) (two r) -> p two c r", p=128, two=2)
                              [:, :, :, j0 * 128:j0 * 128 + m_], QW[sl],
                              waits=wwl)
                    for k in range(2):
                        j = j0 + k
                        na = 128 if j < NJ - 1 else TAILJ
                        bA = j % 2
                        bG = 2 + j % 2
                        mtk = None
                        for dc in range(8):
                            w2_ = [wld] if dc == 0 else ()
                            if dc == 0 and bank[bA]["free"] is not None:
                                w2_ = list(w2_) + [bank[bA]["free"]]
                            mtk = mm(PS[bA][0:na, :],
                                     bfv(base + dc * m_ + k * 128, na),
                                     xt_block(dc)[:, 1 + q4 * 512:
                                                  1 + (q4 + 1) * 512],
                                     start=(dc == 0), stop=(dc == 7),
                                     inc=TE if dc == 7 else None, waits=w2_)
                        bank[bA]["ready"] = mtk
                        mtk = None
                        for dc in range(8):
                            w2_ = () if dc else ([bank[bG]["free"]]
                                                 if bank[bG]["free"] is not None
                                                 else ())
                            mtk = mm(PS[bG][0:na, :],
                                     bfv(base + 8 * m_ + dc * m_ + k * 128, na),
                                     xt_block(dc)[:, 1 + q4 * 512:
                                                  1 + (q4 + 1) * 512],
                                     start=(dc == 0), stop=(dc == 7),
                                     inc=TE, waits=w2_)
                        bank[bG]["ready"] = mtk
                        wg = [bank[bG]["ready"]]
                        if gel_tok is not None:
                            wg.append(gel_tok)
                        gel_tok = act(gsc[0:na, :], PS[bG][0:na, :], AF.Gelu,
                                      bias=B1G[0:na, l * NJ + j:l * NJ + j + 1],
                                      inc=SE, waits=wg)
                        mj = vop("scalar_tensor_tensor",
                                 bfv(M_M + j * 512, 512)[0:na, :],
                                 PS[bA][0:na, :],
                                 B1A[0:na, l * NJ + j:l * NJ + j + 1],
                                 gsc[0:na, :], ALU.add, ALU.mult, inc=VE,
                                 waits=[bank[bA]["ready"], gel_tok])
                        bank[bA]["free"] = mj
                        bank[bG]["free"] = mj
                        mj_toks.append(mj)
                    wag_tok[sl] = mj_toks[-1]
                m_done = mj_toks[-1]
                if "M" in dump and s == 0 and l == 0 and q4 == 0:
                    for j in range(NJ):
                        sdep.append(dma("y", dbg["M"][j * 128:(j + 1) * 128, :],
                                        bfv(M_M + j * 512, 512), QY,
                                        waits=[m_done]))
                TRANSF2 = TRANS.bitcast(F32)
                for tc4 in range(4):
                    tc = q4 * 4 + tc4
                    pb_ = ((PS[4], PS[5]) if tc4 % 2 == 0
                           else (PS[6], TRANSF2))
                    bk_ = ((4, 5) if tc4 % 2 == 0 else (6, None))
                    mtk = None
                    for ic in range(NJ):
                        na = 128 if ic < NJ - 1 else TAILJ
                        for et in range(2):
                            w2_ = [m_done, w2ld] if (ic == 0 and et == 0) else ()
                            if ic == 0:
                                fr = (bank[bk_[et]]["free"] if bk_[et] is not None
                                      else trans_tok["free"])
                                if fr is not None:
                                    w2_ = list(w2_) + [fr]
                            mtk = mm(pb_[et][:, :],
                                     bfv(M_M + ic * 512 + tc4 * 128, 128)
                                     [0:na, :],
                                     bfv(M_W2 + ic * 1024 + et * 512, 512)
                                     [0:na, :],
                                     start=(ic == 0), stop=(ic == NJ - 1),
                                     inc=TE if (ic == NJ - 1 and et == 1)
                                     else None, waits=w2_)
                    a2_ = None
                    for et in range(2):
                        xsl = x_chunk(tc)[:, et * 512:(et + 1) * 512]
                        vop("tensor_add", xsl, pb_[et][:, :], xsl,
                            waits=[mtk, b2b_tok] if et == 0 else ())
                        a2_ = vop("tensor_add", xsl, xsl,
                                  b2b[:, et * 512:(et + 1) * 512], inc=VE)
                        if bk_[et] is not None:
                            bank[bk_[et]]["free"] = a2_
                        else:
                            trans_tok["free"] = a2_
            if dump:
                barrier()
            if "XB2" in dump and s == 0 and l == 0:
                for tc in range(NTC):
                    sdep.append(dma("y", dbg["XB2"][tc * 128:(tc + 1) * 128, :],
                                    x_chunk(tc), QY, waits=[a2_]))
            if dump:
                barrier()
            return sdep

        def phaseF(s):
            sdep = []
            barrier()
            def fbuf(qt):
                return (f32v(M_XPW // 2, 4096) if qt % 2 == 0
                        else f32v(24576, 4096))

            xlt = {}
            odt = {}

            def loadq(qt, waits):
                b = fbuf(qt)
                xl = None
                for t4 in range(4):
                    tc = qt * 4 + t4
                    xl = dma("y", b[:, t4 * 1024:(t4 + 1) * 1024],
                             xpa[tc * 128:(tc + 1) * 128, :], QX,
                             waits=waits if t4 == 0 else ())
                xlt[qt] = xl

            loadq(0, [])
            loadq(1, [])
            for qt in range(4):
                xpw = fbuf(qt)
                xl = xlt[qt]
                adds = None
                for t4 in range(4):
                    tc = qt * 4 + t4
                    adds = vop("tensor_add", xpw[:, t4 * 1024:(t4 + 1) * 1024],
                               xpw[:, t4 * 1024:(t4 + 1) * 1024], x_chunk(tc),
                               inc=VE if t4 == 3 else None,
                               waits=[xl] if t4 == 0 else ())
                xws = [xpw[:, t4 * 1024:(t4 + 1) * 1024] for t4 in range(4)]
                SUMc = [STAT[:, 4 * t4:4 * t4 + 1] for t4 in range(4)]
                VARc = [STAT[:, 16 + 4 * t4:17 + 4 * t4] for t4 in range(4)]
                M2c = [STAT[:, 32 + 4 * t4:33 + 4 * t4] for t4 in range(4)]
                for t4 in range(4):
                    vop("reduce_sum", SUMc[t4], xws[t4], AX.X,
                        waits=[adds] if t4 == 0 else ())
                for t4 in range(4):
                    vop("tensor_mul", f32v(L_TMP // 2, 1024), xws[t4], xws[t4])
                    vop("reduce_sum", VARc[t4], f32v(L_TMP // 2, 1024), AX.X)
                for t4 in range(4):
                    vop("tensor_scalar", M2c[t4], SUMc[t4], SUMc[t4],
                        EPS_LN * D * D, ALU.mult, ALU.subtract)
                for t4 in range(4):
                    vop("tensor_scalar_mul", M2c[t4], M2c[t4], 1.0 / (D * D))
                for t4 in range(4):
                    vop("tensor_scalar_mul", SUMc[t4], SUMc[t4], 1.0 / D)
                for t4 in range(4):
                    vop("tensor_scalar_mul", VARc[t4], VARc[t4], 1.0 / D)
                v3 = None
                for t4 in range(4):
                    v3 = vop("tensor_sub", VARc[t4], VARc[t4], M2c[t4],
                             inc=VE if t4 == 3 else None)
                for t4 in range(4):
                    act(VARc[t4], VARc[t4], AF.Ln,
                        waits=[v3] if t4 == 0 else ())
                l2 = None
                for t4 in range(4):
                    l2 = act(VARc[t4], VARc[t4], AF.Exp, scale=-0.5,
                             inc=SE if t4 == 3 else None)
                vop("tensor_copy", f32v(L_TMP // 2, 512),
                    xws[0][:, 0:512], waits=[l2])
                n1 = None
                for t4 in range(4):
                    n1 = vop("tensor_scalar", xws[t4], xws[t4], SUMc[t4],
                             VARc[t4], ALU.subtract, ALU.mult,
                             inc=VE if t4 == 3 else None)
                ot = None
                for t4 in range(4):
                    tc = qt * 4 + t4
                    ot = dma("g", out[s, tc * 128:(tc + 1) * 128, :],
                             xws[t4], QG, waits=[n1])
                    sdep.append(ot)
                odt[qt] = ot
                if qt + 2 < 4:
                    loadq(qt + 2, [ot])
            return sdep

        final_toks = []
        for s in range(NSEQ):
            toks, sdep = phaseA(s)
            final_toks += sdep
            final_toks += phaseB(s, 0)
            final_toks += phaseB(s, 1)
            final_toks += phaseF(s)

        for tk in final_toks:
            P.wait("y", tk)

        with nc.Block() as blk:
            @blk.gpsimd
            def _(g):
                for fn in P.q["g"]:
                    fn(g)

            @blk.sync
            def _(sy):
                for fn in P.q["y"]:
                    fn(sy)

            @blk.tensor
            def _(t):
                for fn in P.q["t"]:
                    fn(t)

            @blk.vector
            def _(v):
                for fn in P.q["v"]:
                    fn(v)

            @blk.scalar
            def _(s_):
                for fn in P.q["s"]:
                    fn(s_)
    return nc


def kernel(L0_attn_x, L0_attn_qkv_w, L0_attn_qkv_b, L0_attn_proj_w, L0_attn_proj_b,
           L0_ffn_wqkv, L0_ffn_q_scale, L0_ffn_k_scale, L0_ffn_wo, L0_ffn_ln_g,
           L0_ffn_ln_b, L0_ffn_w1, L0_ffn_b1, L0_ffn_w2, L0_ffn_b2):
    consts = make_consts()
    nc = build_core()
    common = {
        "qkvw": np.ascontiguousarray(L0_attn_qkv_w, dtype=np.float32),
        "qkvb": np.ascontiguousarray(L0_attn_qkv_b, dtype=np.float32),
        "projw": np.ascontiguousarray(L0_attn_proj_w, dtype=np.float32),
        "projb": np.ascontiguousarray(L0_attn_proj_b, dtype=np.float32),
        "fwqkv": np.ascontiguousarray(L0_ffn_wqkv, dtype=np.float32),
        "fqs": np.ascontiguousarray(L0_ffn_q_scale, dtype=np.float32),
        "fks": np.ascontiguousarray(L0_ffn_k_scale, dtype=np.float32),
        "fwo": np.ascontiguousarray(L0_ffn_wo, dtype=np.float32),
        "flng": np.ascontiguousarray(L0_ffn_ln_g, dtype=np.float32),
        "flnb": np.ascontiguousarray(L0_ffn_ln_b, dtype=np.float32),
        "fw1": np.ascontiguousarray(L0_ffn_w1, dtype=np.float32),
        "fb1": np.ascontiguousarray(L0_ffn_b1, dtype=np.float32),
        "fw2": np.ascontiguousarray(L0_ffn_w2, dtype=np.float32),
        "fb2": np.ascontiguousarray(L0_ffn_b2, dtype=np.float32),
    }
    common.update(consts)
    x = np.ascontiguousarray(L0_attn_x, dtype=np.float32)
    in_maps = []
    for c in range(NCORE):
        m = dict(common)
        m["xin"] = x[c * NSEQ:(c + 1) * NSEQ]
        in_maps.append(m)
    res = run_bass_kernel_spmd(nc, in_maps, core_ids=list(range(NCORE)))
    return np.concatenate([r["out"] for r in res.results], axis=0)


# version 11
# speedup vs baseline: 1.0150x; 1.0150x over previous
import math
import contextlib
import numpy as np
import ml_dtypes
import concourse.bass as bass
import concourse.mybir as mybir
from concourse.bass_utils import run_bass_kernel_spmd

F32 = mybir.dt.float32
BF16 = mybir.dt.bfloat16
AF = mybir.ActivationFunctionType
ALU = mybir.AluOpType
AX = mybir.AxisListType

NCORE = 8
NSEQ = 4
S = 2048
D = 1024
NTC = S // 128
NDC = D // 128
HD = 64
NHM = 16
NHL = 8
INNER = 512
WIN = 512
HID = 2730
NJ = 22
TAILJ = 42
EPS_LN = 1e-5
XTW = 2064

AR_COLS = 88640
TAB0 = AR_COLS - 4 * 2048


def make_consts():
    c = {}
    c["IDENT"] = np.eye(128, dtype=np.float32).astype(ml_dtypes.bfloat16)
    c["IDF32"] = np.eye(128, dtype=np.float32)
    o2 = np.zeros((128, 2), dtype=np.float32)
    o2[0:64, 0] = 1.0
    o2[64:128, 1] = 1.0
    c["ONES2"] = o2.astype(ml_dtypes.bfloat16)
    c["ONES1"] = np.ones((128, 1), dtype=np.float32).astype(ml_dtypes.bfloat16)
    r = np.arange(128)[:, None]
    i = np.arange(128)[None, :]
    mc = np.where(r <= i, 0.0, -30000.0).astype(np.float32)
    mu = np.where(r >= i, 0.0, -30000.0).astype(np.float32)
    c["MASKC"] = mc.astype(ml_dtypes.bfloat16)
    c["MASKU"] = mu.astype(ml_dtypes.bfloat16)
    half = np.arange(0, HD, 2, dtype=np.float32)
    inv_freq = 1.0 / 10000.0 ** (half / HD)
    t = np.arange(S, dtype=np.float32)
    freqs = t[:, None] * inv_freq[None, :]
    freqs = np.concatenate([freqs, freqs], axis=-1)
    base = (half + 0.4 * HD) / (1.4 * HD)
    power = (t - S // 2) / float(WIN // 2)
    scale = base[None, :] ** power[:, None]
    scale = np.concatenate([scale, scale], axis=-1)
    cos = np.cos(freqs)
    sin = np.sin(freqs)
    sgn = np.ones((S, HD), dtype=np.float32)
    sgn[:, 0:32] = -1.0
    csq = (cos * scale).T
    snq = (sin * scale * sgn).T
    csk = (cos / scale).T
    snk = (sin / scale * sgn).T
    dup = lambda a: np.concatenate([a, a], axis=0).astype(np.float32)
    c["CSQ2"] = dup(csq).astype(ml_dtypes.bfloat16)
    c["SNQ2"] = dup(snq).astype(ml_dtypes.bfloat16)
    c["CSK2"] = dup(csk).astype(ml_dtypes.bfloat16)
    c["SNK2"] = dup(snk).astype(ml_dtypes.bfloat16)
    return c


class Sem:
    def __init__(self, name, step=1):
        self.name, self.step, self.n, self.h = name, step, 0, None


class Prog:
    ENGS = ("t", "v", "s", "g", "y")

    def __init__(self):
        self.q = {k: [] for k in self.ENGS}
        self.waited = {}

    def wait(self, eng, tok):
        if tok is None:
            return
        sem, val = tok
        if val <= 0:
            return
        k = (eng, sem.name)
        if self.waited.get(k, -1) >= val:
            return
        self.waited[k] = val
        self.q[eng].append(lambda e, s=sem, v=val: e.wait_ge(s.h, v))

    def ins(self, eng, fn, inc=None, waits=()):
        for w in waits:
            self.wait(eng, w)
        if inc is not None:
            inc.n += inc.step
            self.q[eng].append(
                lambda e, f=fn, s=inc: f(e).then_inc(s.h, s.step))
            return (inc, inc.n)
        self.q[eng].append(fn)
        return None


def build_core(dump=()):
    nc = bass.Bass("TRN2", target_bir_lowering=False)
    P = Prog()
    dump = set(dump)

    di = {}
    def inp(name, shape, dt=F32):
        di[name] = nc.dram_tensor(name, list(shape), dt, kind="ExternalInput").ap()
        return di[name]

    xin = inp("xin", [NSEQ, S, D])
    qkvw = inp("qkvw", [3 * D, D])
    qkvb = inp("qkvb", [3 * D])
    projw = inp("projw", [D, D])
    projb = inp("projb", [D])
    fwqkv = inp("fwqkv", [2, D, 3 * INNER])
    fqs = inp("fqs", [2, HD])
    fks = inp("fks", [2, HD])
    fwo = inp("fwo", [2, INNER, D])
    flng = inp("flng", [2, D])
    flnb = inp("flnb", [2, D])
    fw1 = inp("fw1", [2, D, 2 * HID])
    fb1 = inp("fb1", [2, 2 * HID])
    fw2 = inp("fw2", [2, HID, D])
    fb2 = inp("fb2", [2, D])
    inp("IDENT", [128, 128], BF16)
    inp("IDF32", [128, 128])
    inp("ONES2", [128, 2], BF16)
    inp("ONES1", [128, 1], BF16)
    inp("MASKC", [128, 128], BF16)
    inp("MASKU", [128, 128], BF16)
    inp("CSQ2", [128, S], BF16)
    inp("SNQ2", [128, S], BF16)
    inp("CSK2", [128, S], BF16)
    inp("SNK2", [128, S], BF16)

    out = nc.dram_tensor("out", [NSEQ, S, D], F32, kind="ExternalOutput").ap()
    xpa = nc.dram_tensor("xpa", [S, D], F32, kind="ExternalOutput" if "xpa" in dump
                         else "Internal").ap()
    dbg = {}
    for tag in dump:
        if tag == "xpa":
            continue
        shape = {
            "XT": [NDC * 128, XTW], "KT": [NDC * 128, S], "V65": [NTC * 128, 1040],
            "QT": [NDC * 128, 512], "OQ": [NDC * 128, 512],
            "QT2": [4 * 128, S], "KT2": [4 * 128, S], "V65L": [NTC * 128, 520],
            "OQL": [4 * 128, 512], "XB": [NTC * 128, D], "HN": [NDC * 128, XTW],
            "M": [NJ * 128, 512], "XB2": [NTC * 128, D],
            "RSUM": [64, 512], "XWIN": [512, D], "STT2": [128, 48],
            "Q2R": [128, S], "RBC": [128, S], "LMU": [128, S], "LRS": [128, S],
            "LROWS": [2, S], "XTR": [NDC * 128, XTW],
        }[tag]
        dbg[tag] = nc.dram_tensor("dbg_" + tag, shape, BF16 if tag not in
                                  ("XB", "XB2", "RSUM", "XWIN", "STT2", "Q2R", "RBC", "LMU", "LRS", "LROWS") else F32,
                                  kind="ExternalOutput").ap()

    es = contextlib.ExitStack()
    with es:
        AR = es.enter_context(nc.sbuf_tensor("AR", [128, AR_COLS], BF16))
        ARF = AR.bitcast(F32)
        XT = es.enter_context(nc.sbuf_tensor("XT", [128, NDC * XTW], BF16))
        IDENT = es.enter_context(nc.sbuf_tensor("sIDENT", [128, 128], BF16))
        IDF32 = es.enter_context(nc.sbuf_tensor("sIDF32", [128, 128], F32))
        ONES2 = es.enter_context(nc.sbuf_tensor("sONES2", [128, 2], BF16))
        ONES1 = es.enter_context(nc.sbuf_tensor("sONES1", [128, 1], BF16))
        MASKC = es.enter_context(nc.sbuf_tensor("sMASKC", [128, 128], BF16))
        MASKU = es.enter_context(nc.sbuf_tensor("sMASKU", [128, 128], BF16))
        QKVB = es.enter_context(nc.sbuf_tensor("QKVB", [128, 24], F32))
        LNG = es.enter_context(nc.sbuf_tensor("LNG", [128, 16], F32))
        LNB = es.enter_context(nc.sbuf_tensor("LNB", [128, 16], F32))
        B1A = es.enter_context(nc.sbuf_tensor("B1A", [128, 44], F32))
        B1G = es.enter_context(nc.sbuf_tensor("B1G", [128, 44], F32))
        QS8 = es.enter_context(nc.sbuf_tensor("QS8", [128, 2], F32))
        KSC = es.enter_context(nc.sbuf_tensor("KSC", [128, 2], F32))
        STAT = es.enter_context(nc.sbuf_tensor("STAT", [128, 128], F32))
        PS = [es.enter_context(nc.psum_tensor(f"PS{i}", [128, 512], F32))
              for i in range(7)]
        PSB = [p.bitcast(BF16) for p in PS]
        TRANS = es.enter_context(nc.psum_tensor("TRANS", [128, 1024], BF16))

        sems = []
        def mksem(name, step=1):
            s_ = Sem(name, step)
            sems.append(s_)
            return s_
        TE = mksem("TE")
        VE = mksem("VE")
        SE = mksem("SE")
        QG = mksem("QG", 16)
        QY = mksem("QY", 16)
        QX = mksem("QX", 16)
        QW = [mksem(f"QW{i}", 16) for i in range(4)]
        for s_ in sems:
            s_.h = es.enter_context(nc.semaphore(s_.name))

        def bfv(col, n):
            return AR[:, col:col + n]

        def f32v(col, n):
            return ARF[:, col:col + n]

        A_KT = 0
        A_V65 = 16384
        A_WQT = 33024
        A_WPT = 41216
        A_QTL = 49408
        A_OQ = 53504
        A_PR = 57600
        A_XW = 60672
        A_SQ = 68864
        A_WB = 70912
        A_BV = 72960
        A_PB = 75008
        A_RB = 77056
        A_XB2 = 78080
        assert A_XB2 + 2048 <= TAB0

        B_X = 0
        B_QT2 = 33024
        B_KT2 = 41216
        B_V65 = 49408
        B_PR = 57728
        B_OQL = 60800
        B_Q2R = 62848
        B_SQB = 66944
        B_WB = 68992
        B_RB = 73088
        B_RBC = 74112
        assert B_RBC + 4096 <= TAB0
        L_MBC = 32768
        L_RBC = 36864
        L_TMP = 40960
        L_SQT = 45056
        L_SQT2 = 72704
        B_XBN = 74112
        M_M = 32768
        M_WAG = 44032
        M_W2 = 48128
        M_B2B = 70656
        M_GSC = 72704
        M_XPW = 32768
        assert M_W2 + 22528 <= M_B2B
        assert M_B2B + 2048 <= L_SQT2 + 1024 + 2048

        CSQ2 = bfv(TAB0 + 0 * S, S)
        SNQ2 = bfv(TAB0 + 1 * S, S)
        CSK2 = bfv(TAB0 + 2 * S, S)
        SNK2 = bfv(TAB0 + 3 * S, S)

        def dma(q, out_ap, in_ap, sem, waits=()):
            return P.ins(q, lambda e: e.dma_start(out=out_ap, in_=in_ap),
                         inc=sem, waits=waits)

        def dma_slow(q, out_ap, in_ap, sem, waits=()):
            return P.ins(q, lambda e: e.dma_start(
                out=out_ap, in_=in_ap, allow_slow_non_contiguous=True),
                inc=sem, waits=waits)

        def mm(out_ap, lhsT, rhs, start, stop, inc=None, waits=(), tp=None):
            return P.ins("t", lambda e: e.matmul(
                out_ap, lhsT, rhs, start=start, stop=stop, tile_position=tp),
                inc=inc, waits=waits)

        def transp(out_ap, in_ap, ident, inc=None, waits=()):
            return P.ins("t", lambda e: e.transpose(out_ap, in_ap, ident),
                         inc=inc, waits=waits)

        def act(out_ap, in_ap, func, bias=None, scale=None, accum=None,
                inc=None, waits=()):
            kw = {}
            if bias is not None:
                kw["bias"] = bias
            if scale is not None:
                kw["scale"] = scale
            if accum is not None:
                kw["accum_out"] = accum
            return P.ins("s", lambda e: e.activation(out_ap, in_ap, func, **kw),
                         inc=inc, waits=waits)

        def vop(name, *args, inc=None, waits=()):
            return P.ins("v", lambda e: getattr(e, name)(*args),
                         inc=inc, waits=waits)

        bank = [{"ready": None, "free": None} for _ in range(7)]
        trans_tok = {"ready": None, "free": None}
        sq_fence = [None]

        cwait = []
        for nm, tgt in [("IDENT", IDENT), ("IDF32", IDF32), ("ONES2", ONES2),
                        ("ONES1", ONES1), ("MASKC", MASKC), ("MASKU", MASKU)]:
            cwait.append(dma("g", tgt[:, :], di[nm], QG))
        for nm, col in [("CSQ2", TAB0), ("SNQ2", TAB0 + S),
                        ("CSK2", TAB0 + 2 * S), ("SNK2", TAB0 + 3 * S)]:
            cwait.append(dma("g", bfv(col, S), di[nm], QG))
        cwait.append(dma_slow("g", QKVB[:, 0:24],
                              di["qkvb"].rearrange("(c p) -> p c", p=128), QG))
        for l in range(2):
            cwait.append(dma_slow("g", LNG[:, l * 8:(l + 1) * 8],
                                  di["flng"][l].rearrange("(c p) -> p c", p=128), QG))
            cwait.append(dma_slow("g", LNB[:, l * 8:(l + 1) * 8],
                                  di["flnb"][l].rearrange("(c p) -> p c", p=128), QG))
            for j in range(NJ):
                na = 128 if j < NJ - 1 else TAILJ
                cwait.append(dma("g", B1A[0:na, l * NJ + j:l * NJ + j + 1],
                                 di["fb1"][l, j * 128:j * 128 + na].rearrange("(p one) -> p one", one=1), QG))
                cwait.append(dma("g", B1G[0:na, l * NJ + j:l * NJ + j + 1],
                                 di["fb1"][l, HID + j * 128:HID + j * 128 + na]
                                 .rearrange("(p one) -> p one", one=1), QG))
            for v_, src in [(QS8, "fqs"), (KSC, "fks")]:
                cwait.append(dma("g", v_[0:64, l:l + 1],
                                 di[src][l].rearrange("(p one) -> p one", one=1), QG))
                cwait.append(dma("g", v_[64:128, l:l + 1],
                                 di[src][l].rearrange("(p one) -> p one", one=1), QG))
        const_tok = cwait[-1]
        t_qs = vop("tensor_scalar_mul", QS8[:, :], QS8[:, :], 8.0,
                   inc=VE, waits=[const_tok])
        prolog_tok = t_qs

        def build_bias_bcasts(gate):
            toks = []
            last = gate
            for dst, src in ((A_BV, ("qkvb", 2 * D)), (A_PB, ("projb", 0))):
                nm, off = src
                for half in range(2):
                    tk1 = dma("y", ARF[0:1, A_RB // 2:A_RB // 2 + 512],
                              di[nm][off + half * 512: off + (half + 1) * 512]
                              .rearrange("(one x) -> one x", one=1), QY,
                              waits=[last] if last else ())
                    for b32 in range(4):
                        last = vop(
                            "stream_shuffle",
                            f32v(dst // 2 + half * 512, 512)[b32 * 32:(b32 + 1) * 32, :],
                            ARF[0:32, A_RB // 2:A_RB // 2 + 512], [0] * 32,
                            inc=VE, waits=[tk1])
                    toks.append(last)
            return toks[-1]

        xtpad_tok = None
        for dc in range(8):
            xtpad_tok = P.ins("v", lambda e, c=dc: e.memset(
                XT[:, c * XTW:c * XTW + 1], 0.0), inc=VE if dc == 7 else None)

        def xt_block(dc):
            return XT[:, dc * XTW: (dc + 1) * XTW]

        def build_T_from_nat(nat_tiles, dst_block_fn, ident, nat_tok_fn,
                             extra_free=None):
            toks = []
            for tc, ap in nat_tiles:
                w = [nat_tok_fn(tc)]
                if trans_tok["free"] is not None:
                    w.append(trans_tok["free"])
                for dc in range(8):
                    tk = transp(TRANS[:, dc * 128:(dc + 1) * 128],
                                ap[:, dc * 128:(dc + 1) * 128], ident,
                                inc=TE if dc == 7 else None, waits=w if dc == 0 else ())
                trans_tok["ready"] = tk
                etoks = []
                for dc in range(8):
                    etk = vop("tensor_copy",
                              dst_block_fn(dc)[:, 1 + tc * 128:1 + (tc + 1) * 128],
                              TRANS[:, dc * 128:(dc + 1) * 128],
                              inc=VE if dc == 7 else None,
                              waits=[trans_tok["ready"]] if dc == 0 else ())
                trans_tok["free"] = etk
                toks.append(etk)
            return toks

        def phaseA(s):
            sdep = []
            barrier()
            pending_xt = []

            def flush_xt_writeback():
                while pending_xt:
                    tc, sl, cb = pending_xt.pop(0)
                    w = [cb]
                    if trans_tok["free"] is not None:
                        w.append(trans_tok["free"])
                    for dc in range(8):
                        sv = bfv((A_WB if sl < 2 else A_XB2 - 2048)
                                 + sl * 1024 + dc * 128, 128)
                        tk = transp(TRANS[:, dc * 128:(dc + 1) * 128],
                                    sv, IDENT[:, :],
                                    inc=TE if dc == 7 else None,
                                    waits=w if dc == 0 else ())
                    trans_tok["ready"] = tk
                    wb_slot[sl] = tk
                    etk = None
                    for dc in range(8):
                        etk = vop("tensor_copy",
                                  xt_block(dc)[:, 1 + tc * 128:
                                               1 + (tc + 1) * 128],
                                  TRANS[:, dc * 128:(dc + 1) * 128],
                                  inc=VE if dc == 7 else None,
                                  waits=[trans_tok["ready"]] if dc == 0 else ())
                    trans_tok["free"] = etk
                    ln_sp_toks.append(etk)
            bias_tok = build_bias_bcasts(None)

            ones_ap = bfv(A_V65, 16 * 1040).rearrange(
                "p (t h e) -> p (t h) e", h=16, e=65)[:, :, 64:65]

            xb_toks = {}
            for tc in range(NTC):
                xb_toks[tc] = dma("g", bfv(A_V65 + tc * 1024, 1024),
                                  xin[s, tc * 128:(tc + 1) * 128, :], QG)
            xb_all = xb_toks[NTC - 1]
            build_T_from_nat([(tc, bfv(A_V65 + tc * 1024, 1024)) for tc in range(NTC)],
                             xt_block, IDENT[:, :], lambda tc: xb_all)
            xt_ready = trans_tok["free"]

            def stream_wt(oc_list, dst_fn, bias_col_fn, src_row0, tag):
                slot_tok = [None, None]
                for idx, oc in enumerate(oc_list):
                    sl = idx % 2
                    wb = bfv(A_WB + sl * 1024, 1024)
                    wld = dma("g", wb, di["qkvw"][src_row0 + oc * 128:
                                                  src_row0 + (oc + 1) * 128, :],
                              QW[sl], waits=[slot_tok[sl]] if slot_tok[sl] else ())
                    wb2 = bfv(A_WB + sl * 1024, 1024)
                    w = [wld]
                    if trans_tok["free"] is not None:
                        w.append(trans_tok["free"])
                    for dc in range(8):
                        tk = transp(TRANS[:, dc * 128:(dc + 1) * 128],
                                    wb[:, dc * 128:(dc + 1) * 128], IDENT[:, :],
                                    inc=TE if dc == 7 else None,
                                    waits=w if dc == 0 else ())
                    trans_tok["ready"] = tk
                    etk = None
                    for dc in range(8):
                        etk = vop("tensor_copy", wb2[:, dc * 128:(dc + 1) * 128],
                                  TRANS[:, dc * 128:(dc + 1) * 128],
                                  inc=VE if dc == 7 else None,
                                  waits=[trans_tok["ready"]] if dc == 0 else ())
                    trans_tok["free"] = etk
                    mtk = None
                    for dc in range(8):
                        for tt in range(4):
                            w2 = [etk, xt_ready] if (dc == 0 and tt == 0) else ()
                            if dc == 0 and bank[tt]["free"] is not None:
                                w2 = list(w2) + [bank[tt]["free"]]
                            mtk = mm(PS[tt][:, :], wb2[:, dc * 128:(dc + 1) * 128],
                                     xt_block(dc)[:, 1 + tt * 512:1 + (tt + 1) * 512],
                                     start=(dc == 0), stop=(dc == 7),
                                     inc=TE if (dc == 7 and tt == 3) else None,
                                     waits=w2)
                    ev = None
                    for tt in range(4):
                        ev = act(dst_fn(oc)[:, tt * 512:(tt + 1) * 512],
                                 PS[tt][:, :], AF.Identity,
                                 bias=QKVB[:, bias_col_fn(oc):bias_col_fn(oc) + 1],
                                 scale=1.0, inc=SE,
                                 waits=[mtk] if tt == 0 else ())
                        bank[tt]["ready"] = mtk
                        bank[tt]["free"] = ev
                    slot_tok[sl] = ev
                return slot_tok

            stream_wt(list(range(8)), lambda oc: bfv(A_KT + oc * 2048, 2048),
                      lambda oc: 8 + oc, D, "K")
            kt_done = bank[0]["free"]
            kt_done2 = bank[1]["free"]

            wvt_toks = []
            for oc in range(8):
                sl = oc % 2
                wb = bfv(A_WB + sl * 1024, 1024)
                wld = dma("g", wb, di["qkvw"][2 * D + oc * 128:2 * D + (oc + 1) * 128, :],
                          QW[sl], waits=[bank[0]["free"], bank[1]["free"]]
                          if oc < 2 else [wvt_toks[oc - 2]])
                w = [wld, trans_tok["free"]]
                for dc in range(8):
                    tk = transp(TRANS[:, dc * 128:(dc + 1) * 128],
                                wb[:, dc * 128:(dc + 1) * 128], IDENT[:, :],
                                inc=TE if dc == 7 else None, waits=w if dc == 0 else ())
                trans_tok["ready"] = tk
                etk = None
                for dc in range(8):
                    etk = vop("tensor_copy",
                              bfv(A_QTL + dc * 1024 + oc * 128, 128),
                              TRANS[:, dc * 128:(dc + 1) * 128],
                              inc=VE if dc == 7 else None,
                              waits=[trans_tok["ready"]] if dc == 0 else ())
                trans_tok["free"] = etk
                wvt_toks.append(etk)
            wvt_done = wvt_toks[-1]

            ones_tok = vop("memset", ones_ap, 1.0, inc=VE, waits=[xt_ready])
            v_done = []
            for tc in range(NTC):
                bb = 2 + 2 * (tc % 2)
                mtk = None
                for dc in range(8):
                    for ot in range(2):
                        w2 = [wvt_done] if (dc == 0 and ot == 0) else ()
                        if dc == 0 and bank[bb + ot]["free"] is not None:
                            w2 = list(w2) + [bank[bb + ot]["free"]]
                        mtk = mm(PS[bb + ot][:, :],
                                 xt_block(dc)[:, 1 + tc * 128:1 + (tc + 1) * 128],
                                 bfv(A_QTL + dc * 1024 + ot * 512, 512),
                                 start=(dc == 0), stop=(dc == 7),
                                 inc=TE if (dc == 7 and ot == 1) else None,
                                 waits=w2)
                ev = None
                for ot in range(2):
                    bank[bb + ot]["ready"] = mtk
                    for h in range(8):
                        gh = ot * 8 + h
                        ev = vop("tensor_add",
                                 bfv(A_V65 + tc * 1040 + gh * 65, 64),
                                 PS[bb + ot][:, h * 64:(h + 1) * 64],
                                 f32v(A_BV // 2 + ot * 512 + h * 64, 64),
                                 inc=VE if h == 7 else None,
                                 waits=[mtk, ones_tok] if (ot == 0 and h == 0)
                                 else ())
                    bank[bb + ot]["free"] = ev
                    v_done.append(ev)
            v65_ready = v_done[-1]

            wqt_toks = []
            for oc in range(8):
                sl = oc % 2
                wb = bfv(A_WB + sl * 1024, 1024)
                wld = dma("g", wb, di["qkvw"][oc * 128:(oc + 1) * 128, :],
                          QW[sl], waits=[wvt_toks[-1]] if oc < 2 else [wqt_toks[oc - 2]])
                w = [wld, trans_tok["free"]]
                for dc in range(8):
                    tk = transp(TRANS[:, dc * 128:(dc + 1) * 128],
                                wb[:, dc * 128:(dc + 1) * 128], IDENT[:, :],
                                inc=TE if dc == 7 else None, waits=w if dc == 0 else ())
                trans_tok["ready"] = tk
                etk = None
                for dc in range(8):
                    etk = vop("tensor_copy",
                              bfv(A_WQT + dc * 1024 + oc * 128, 128),
                              TRANS[:, dc * 128:(dc + 1) * 128],
                              inc=VE if dc == 7 else None,
                              waits=[trans_tok["ready"]] if dc == 0 else ())
                trans_tok["free"] = etk
                wqt_toks.append(etk)
            wqt_done = wqt_toks[-1]

            wpt_toks = []
            for ec in range(8):
                sl = ec % 2
                wb = bfv(A_WB + sl * 1024, 1024)
                wld = dma("g", wb, di["projw"][ec * 128:(ec + 1) * 128, :],
                          QW[sl], waits=[wqt_toks[-1]] if ec < 2 else [wpt_toks[ec - 2]])
                w = [wld, trans_tok["free"]]
                for oc in range(8):
                    tk = transp(TRANS[:, oc * 128:(oc + 1) * 128],
                                wb[:, oc * 128:(oc + 1) * 128], IDENT[:, :],
                                inc=TE if oc == 7 else None, waits=w if oc == 0 else ())
                trans_tok["ready"] = tk
                etk = None
                for oc in range(8):
                    etk = vop("tensor_copy",
                              bfv(A_WPT + oc * 1024 + ec * 128, 128),
                              TRANS[:, oc * 128:(oc + 1) * 128],
                              inc=VE if oc == 7 else None,
                              waits=[trans_tok["ready"]] if oc == 0 else ())
                trans_tok["free"] = etk
                wpt_toks.append(etk)
            wpt_done = wpt_toks[-1]

            if "XT" in dump and s == 0:
                for dc in range(8):
                    sdep.append(dma("y", dbg["XT"][dc * 128:(dc + 1) * 128, :],
                                    xt_block(dc), QY, waits=[xt_ready]))
            if "KT" in dump and s == 0:
                for oc in range(8):
                    sdep.append(dma("y", dbg["KT"][oc * 128:(oc + 1) * 128, :],
                                    bfv(A_KT + oc * 2048, 2048), QY,
                                    waits=[kt_done, kt_done2]))
            if "V65" in dump and s == 0:
                for tc in range(NTC):
                    sdep.append(dma("y", dbg["V65"][tc * 128:(tc + 1) * 128, :],
                                    bfv(A_V65 + tc * 1040, 1040), QY,
                                    waits=[v65_ready]))

            ln_sp_toks = []
            xw_free = [None] * 4
            wb_slot = [None, None, None, None]
            for qt in range(4):
                for tc4 in range(4):
                    tc = qt * 4 + tc4
                    xw_all = dma("g", f32v(A_XW // 2 + tc4 * 1024, 1024),
                                 xin[s, tc * 128:(tc + 1) * 128, :], QX,
                                 waits=[xw_free[tc4]] if xw_free[tc4] else ())
                for oc in range(8):
                    b = oc % 2
                    mtk = None
                    for dc in range(8):
                        w2 = [wqt_done, xt_ready] if dc == 0 else ()
                        if dc == 0 and bank[b]["free"] is not None:
                            w2 = list(w2) + [bank[b]["free"]]
                        mtk = mm(PS[b][:, :], bfv(A_WQT + dc * 1024 + oc * 128, 128),
                                 xt_block(dc)[:, 1 + qt * 512:1 + (qt + 1) * 512],
                                 start=(dc == 0), stop=(dc == 7),
                                 inc=TE if dc == 7 else None, waits=w2)
                    bank[b]["ready"] = mtk
                    ev = act(bfv(A_QTL + oc * 512, 512), PS[b][:, :], AF.Identity,
                             bias=QKVB[:, oc:oc + 1], scale=1.0, inc=SE,
                             waits=[bank[b]["ready"]])
                    bank[b]["free"] = ev
                qtl_ready = bank[0]["free"]
                qtl_ready2 = bank[1]["free"]

                if "QT" in dump and s == 0 and qt == 0:
                    for oc in range(8):
                        sdep.append(dma("y", dbg["QT"][oc * 128:(oc + 1) * 128, :],
                                        bfv(A_QTL + oc * 512, 512), QY,
                                        waits=[qtl_ready, qtl_ready2]))

                TRANSF = TRANS.bitcast(F32)
                pctr = [0]
                pslot_tok = [None] * 6
                for pg in range(4):
                    prs = (2 * pg, 2 * pg + 1)
                    sbank = {prs[0]: (0, 1), prs[1]: (2, 3)}
                    obank = {prs[0]: (PS[4], PS[5]), prs[1]: (PS[6], TRANSF)}
                    exp_prev = {p: [None, None] for p in prs}
                    exp_cur = {p: [None, None] for p in prs}
                    avts = {p: [None, None] for p in prs}
                    for kc in range(16):
                        for p in prs:
                            b0, b1 = sbank[p]
                            wS = [qtl_ready, qtl_ready2, kt_done, kt_done2]
                            if bank[b0]["free"] is not None:
                                wS.append(bank[b0]["free"])
                            mm(PS[b0][:, :],
                               bfv(A_KT + p * 2048 + kc * 128, 128)[0:64, :],
                               bfv(A_QTL + p * 512, 512)[0:64, :],
                               start=True, stop=True, waits=wS)
                            wS2 = ([bank[b1]["free"]]
                                   if bank[b1]["free"] is not None else ())
                            t2 = mm(PS[b1][:, :],
                                    bfv(A_KT + p * 2048 + kc * 128, 128)[64:128, :],
                                    bfv(A_QTL + p * 512, 512)[64:128, :],
                                    start=True, stop=True, inc=TE, waits=wS2)
                            bank[b0]["ready"] = t2
                            bank[b1]["ready"] = t2
                            for hh, bx in ((0, b0), (1, b1)):
                                slot = pctr[0] % 6
                                pctr[0] += 1
                                wE = [t2]
                                if pslot_tok[slot] is not None:
                                    wE.append(pslot_tok[slot])
                                ek = act(bfv(A_PR + slot * 512, 512),
                                         PS[bx][:, :], AF.Exp, scale=0.125,
                                         inc=SE, waits=wE)
                                exp_cur[p][hh] = (ek, slot)
                                bank[bx]["free"] = ek
                            if kc > 0:
                                for hh in range(2):
                                    ekp, slot = exp_prev[p][hh]
                                    ob = obank[p][hh]
                                    wA = [ekp]
                                    if kc == 1:
                                        wA.append(v65_ready)
                                        if p == prs[1]:
                                            fr = (bank[6]["free"] if hh == 0
                                                  else trans_tok["free"])
                                        else:
                                            fr = bank[4 + hh]["free"]
                                        if fr is not None:
                                            wA.append(fr)
                                    avts[p][hh] = mm(
                                        ob[0:65, :],
                                        bfv(A_V65 + (kc - 1) * 1040 +
                                            (p * 2 + hh) * 65, 65),
                                        bfv(A_PR + slot * 512, 512),
                                        start=(kc == 1), stop=False,
                                        inc=TE, waits=wA)
                                    pslot_tok[slot] = avts[p][hh]
                            exp_prev[p] = exp_cur[p]
                            exp_cur[p] = [None, None]
                    for p in prs:
                        for hh in range(2):
                            ekp, slot = exp_prev[p][hh]
                            avts[p][hh] = mm(obank[p][hh][0:65, :],
                                             bfv(A_V65 + 15 * 1040 +
                                                 (p * 2 + hh) * 65, 65),
                                             bfv(A_PR + slot * 512, 512),
                                             start=False, stop=True,
                                             inc=TE, waits=[ekp])
                            pslot_tok[slot] = avts[p][hh]
                    for p in prs:
                        o0, o1 = obank[p]
                        rbw = ARF[:, A_RB // 2:A_RB // 2 + 512]
                        vop("tensor_copy", rbw[0:1, :], o0[64:65, :],
                            waits=[avts[p][0], sq_fence[0]])
                        rc = vop("tensor_copy", rbw[32:33, :], o1[64:65, :],
                                 inc=VE, waits=[avts[p][1]])
                        act(rbw[0:33, :], rbw[0:33, :], AF.Ln, waits=[rc])
                        rc2 = act(rbw[0:33, :], rbw[0:33, :], AF.Exp,
                                  scale=-1.0, inc=SE)
                        sh = None
                        for b32 in range(4):
                            srow = 0 if b32 < 2 else 32
                            sh = vop("stream_shuffle",
                                     f32v(A_SQ // 2, 512)
                                     [b32 * 32:(b32 + 1) * 32, :],
                                     rbw[srow:srow + 32, :], [0] * 32,
                                     inc=VE if b32 == 3 else None,
                                     waits=[rc2] if b32 == 0 else ())
                        vop("tensor_mul", bfv(A_OQ + p * 512, 512)[0:64, :],
                            o0[0:64, :], f32v(A_SQ // 2, 512)[0:64, :],
                            waits=[sh])
                        m2 = vop("tensor_mul",
                                 bfv(A_OQ + p * 512, 512)[64:128, :],
                                 o1[0:64, :], f32v(A_SQ // 2, 512)[64:128, :],
                                 inc=VE)
                        sq_fence[0] = m2
                        if p == prs[0]:
                            bank[4]["free"] = m2
                            bank[5]["free"] = m2
                        else:
                            bank[6]["free"] = m2
                            trans_tok["free"] = m2
                oq_ready = m2
                flush_xt_writeback()

                if "OQ" in dump and s == 0 and qt == 0:
                    for oc in range(8):
                        sdep.append(dma("y", dbg["OQ"][oc * 128:(oc + 1) * 128, :],
                                        bfv(A_OQ + oc * 512, 512), QY,
                                        waits=[oq_ready]))

                for tc4 in range(4):
                    tc = qt * 4 + tc4
                    b0 = (tc4 % 2) * 2
                    mtk = None
                    for oc in range(8):
                        for et in range(2):
                            w2 = [oq_ready] if (oc == 0 and et == 0) else ()
                            if oc == 0 and bank[b0 + et]["free"] is not None:
                                w2 = list(w2) + [bank[b0 + et]["free"]]
                            mtk = mm(PS[b0 + et][:, :],
                                     bfv(A_OQ + oc * 512 + tc4 * 128, 128),
                                     bfv(A_WPT + oc * 1024 + et * 512, 512),
                                     start=(oc == 0), stop=(oc == 7),
                                     inc=TE if (oc == 7 and et == 1) else None,
                                     waits=w2)
                    a2 = None
                    for et in range(2):
                        b = b0 + et
                        bank[b]["ready"] = mtk
                        xw = f32v(A_XW // 2 + tc4 * 1024 + et * 512, 512)
                        vop("tensor_add", xw, PS[b][:, :], xw,
                            waits=[mtk, xw_all] if et == 0 else ())
                        a2 = vop("tensor_add", xw, xw,
                                 f32v(A_PB // 2 + et * 512, 512), inc=VE)
                        bank[b]["free"] = a2
                    if "XWIN" in dump and s == 0 and qt == 0:
                        sdep.append(dma("y", dbg["XWIN"][tc4 * 128:(tc4 + 1) * 128, :],
                                        f32v(A_XW // 2 + tc4 * 1024, 1024), QY,
                                        waits=[a2]))

                xws = [f32v(A_XW // 2 + t4 * 1024, 1024) for t4 in range(4)]
                SUMc = [STAT[:, 4 * t4:4 * t4 + 1] for t4 in range(4)]
                VARc = [STAT[:, 16 + 4 * t4:17 + 4 * t4] for t4 in range(4)]
                M2c = [STAT[:, 32 + 4 * t4:33 + 4 * t4] for t4 in range(4)]
                r1 = None
                for t4 in range(4):
                    r1 = vop("reduce_sum", SUMc[t4], xws[t4], AX.X,
                             inc=VE if t4 == 3 else None,
                             waits=[a2] if t4 == 0 else ())
                for t4 in range(4):
                    vop("tensor_mul", f32v(A_SQ // 2, 1024), xws[t4], xws[t4],
                        waits=[sq_fence[0]] if t4 == 0 else ())
                    vop("reduce_sum", VARc[t4], f32v(A_SQ // 2, 1024), AX.X)
                for t4 in range(4):
                    vop("tensor_scalar", M2c[t4], SUMc[t4], SUMc[t4],
                        EPS_LN * D * D, ALU.mult, ALU.subtract)
                for t4 in range(4):
                    vop("tensor_scalar_mul", M2c[t4], M2c[t4], 1.0 / (D * D))
                for t4 in range(4):
                    vop("tensor_scalar_mul", SUMc[t4], SUMc[t4], 1.0 / D)
                for t4 in range(4):
                    vop("tensor_scalar_mul", VARc[t4], VARc[t4], 1.0 / D)
                v3 = None
                for t4 in range(4):
                    v3 = vop("tensor_sub", VARc[t4], VARc[t4], M2c[t4],
                             inc=VE if t4 == 3 else None)
                sq_fence[0] = v3
                for t4 in range(4):
                    act(VARc[t4], VARc[t4], AF.Ln,
                        waits=[v3] if t4 == 0 else ())
                l2 = None
                for t4 in range(4):
                    l2 = act(VARc[t4], VARc[t4], AF.Exp, scale=-0.5,
                             inc=SE if t4 == 3 else None)
                if "STT2" in dump and s == 0 and qt == 0:
                    sdep.append(dma("y", dbg["STT2"][:, 0:48],
                                    STAT[:, 0:48], QY, waits=[l2]))
                vop("tensor_copy", f32v(A_SQ // 2, 512), xws[0][:, 0:512],
                    waits=[l2])
                n1 = None
                for t4 in range(4):
                    n1 = vop("tensor_scalar", xws[t4], xws[t4], SUMc[t4],
                             VARc[t4], ALU.subtract, ALU.mult,
                             inc=VE if t4 == 3 else None)
                for t4 in range(4):
                    tc = qt * 4 + t4
                    sl = t4
                    wcb = [n1] if t4 == 0 else []
                    if wb_slot[sl] is not None:
                        wcb.append(wb_slot[sl])
                    dst = bfv((A_WB if sl < 2 else A_XB2 - 2048)
                              + sl * 1024, 1024)
                    cb = act(dst, xws[t4], AF.Copy,
                             inc=SE, waits=wcb)
                    sp = dma("y", xpa[tc * 128:(tc + 1) * 128, :], xws[t4], QY,
                             waits=[cb])
                    ln_sp_toks.append(sp)
                    xw_free[t4] = sp
                    pending_xt.append((tc, sl, cb))
                if qt == 3:
                    flush_xt_writeback()
            return ln_sp_toks, sdep

        allsems = [TE, VE, SE, QG, QY, QX] + QW

        def barrier():
            snap = [(s_, s_.n) for s_ in allsems if s_.n > 0]
            for e in Prog.ENGS:
                for tk in snap:
                    P.wait(e, tk)

        def x_chunk(tc):
            return f32v(tc * 1024, 1024)

        def xt_shift_cols(dc, c0, n):
            off = 1 if dc < 4 else 0
            return xt_block(dc)[:, off + c0: off + c0 + n]

        def rebuild_xt(s, entry=None):
            wb_slot = [None, None]
            for tc in range(NTC):
                sl = tc % 2
                wcb = [wb_slot[sl]] if wb_slot[sl] else []
                if tc == 0 and entry is not None:
                    wcb.append(entry)
                cb = act(bfv(B_XBN + sl * 1024, 1024), x_chunk(tc), AF.Copy,
                         inc=SE, waits=wcb)
                w = [cb]
                if trans_tok["free"] is not None:
                    w.append(trans_tok["free"])
                for dc in range(8):
                    tk = transp(TRANS[:, dc * 128:(dc + 1) * 128],
                                bfv(B_XBN + sl * 1024 + dc * 128, 128),
                                IDENT[:, :], inc=TE if dc == 7 else None,
                                waits=w if dc == 0 else ())
                trans_tok["ready"] = tk
                wb_slot[sl] = tk
                etk = None
                for dc in range(8):
                    etk = vop("tensor_copy",
                              xt_block(dc)[:, 1 + tc * 128:1 + (tc + 1) * 128],
                              TRANS[:, dc * 128:(dc + 1) * 128],
                              inc=VE if dc == 7 else None,
                              waits=[trans_tok["ready"]] if dc == 0 else ())
                trans_tok["free"] = etk
            return etk

        def phaseB(s, l):
            sdep = []
            barrier()
            if l == 0:
                xl = None
                for tc in range(NTC):
                    xl = dma("y", x_chunk(tc), xpa[tc * 128:(tc + 1) * 128, :],
                             QY)
                xload = xl
            else:
                xload = rebuild_xt(s)

            ones_ap = bfv(B_V65, 16 * 520).rearrange(
                "p (t h e) -> p (t h) e", h=8, e=65)[:, :, 64:65]
            ones_tok = vop("memset", ones_ap, 1.0, waits=[xload])
            wvb = bfv(B_PR, 8 * 512)
            wvld = dma("g", wvb.rearrange("p (c m) -> p c m", c=8),
                       di["fwqkv"][l].rearrange("(c p) m -> p c m", p=128)
                       [:, :, 1024:1536], QW[2], waits=[xload])

            def v2_chunk(tc):
                b = 4 + tc % 2
                mtk = None
                for dc in range(8):
                    w2 = [wvld, xload] if dc == 0 else ()
                    if dc == 0 and bank[b]["free"] is not None:
                        w2 = list(w2) + [bank[b]["free"]]
                    mtk = mm(PS[b][:, :], xt_shift_cols(dc, tc * 128, 128),
                             wvb[:, dc * 512:(dc + 1) * 512],
                             start=(dc == 0), stop=(dc == 7),
                             inc=TE if dc == 7 else None, waits=w2)
                bank[b]["ready"] = mtk
                ev = None
                for h in range(8):
                    ev = vop("tensor_copy",
                             bfv(B_V65 + tc * 520 + h * 65, 64),
                             PS[b][:, h * 64:(h + 1) * 64],
                             inc=VE if h == 7 else None,
                             waits=[mtk, ones_tok] if h == 0 else ())
                bank[b]["free"] = ev
                return ev

            q2r = f32v(B_Q2R // 2, 2048)
            sqb = bfv(B_SQB, 2048)
            rbrow = ARF[:, B_RB // 2:B_RB // 2 + 512]
            rbc = f32v(B_RBC // 2, 2048)
            slot_tok = [None, None]
            qk_tok = None
            for oc in range(8):
                sl = oc % 2
                wb = bfv(B_WB + sl * 1024, 1024)
                wld = dma("g", wb.rearrange("p (c m) -> p c m", c=8),
                          di["fwqkv"][l].rearrange("(c p) m -> p c m", p=128)
                          [:, :, oc * 128:(oc + 1) * 128], QW[sl],
                          waits=[slot_tok[sl], xload] if slot_tok[sl]
                          else [xload])
                mtk = None
                for dc in range(8):
                    for tt in range(4):
                        w2 = [wld] if (dc == 0 and tt == 0) else ()
                        if dc == 0 and bank[tt]["free"] is not None:
                            w2 = list(w2) + [bank[tt]["free"]]
                        mtk = mm(PS[tt][:, :], wb[:, dc * 128:(dc + 1) * 128],
                                 xt_shift_cols(dc, tt * 512, 512),
                                 start=(dc == 0), stop=(dc == 7),
                                 inc=TE if (dc == 7 and tt == 3) else None,
                                 waits=w2)
                v_done = v2_chunk(2 * oc)
                v_done = v2_chunk(2 * oc + 1)
                ev = None
                for tt in range(4):
                    bank[tt]["ready"] = mtk
                    wev = [mtk] if tt == 0 else []
                    if tt == 0 and qk_tok is not None:
                        wev.append(qk_tok)
                    ev = act(q2r[:, tt * 512:(tt + 1) * 512], PS[tt][:, :],
                             AF.Identity, scale=1.0, inc=SE, waits=wev)
                    bank[tt]["free"] = ev
                slot_tok[sl] = ev
                if "Q2R" in dump and s == 0 and l == 0 and oc == 0:
                    sdep.append(dma("y", dbg["Q2R"], q2r, QY, waits=[ev]))
                    barrier()
                sq1 = vop("tensor_mul", sqb, q2r, q2r, inc=VE, waits=[ev])
                for tt in range(4):
                    wS = [sq1]
                    if bank[6]["free"] is not None:
                        wS.append(bank[6]["free"])
                    smk = mm(PS[6][0:2, :], ONES2[:, :],
                             sqb[:, tt * 512:(tt + 1) * 512],
                             start=True, stop=True, inc=TE, waits=wS)
                    bank[6]["ready"] = smk
                    ec = vop("tensor_copy", rbrow[0:2, :], PS[6][0:2, :],
                             inc=VE, waits=[smk])
                    bank[6]["free"] = ec
                    c1 = vop("tensor_scalar_max", rbrow[0:2, :], rbrow[0:2, :],
                             1e-24, inc=VE)
                    act(rbrow[0:2, :], rbrow[0:2, :], AF.Ln, waits=[c1])
                    a4 = act(rbrow[0:2, :], rbrow[0:2, :], AF.Exp,
                             scale=-0.5, inc=SE)
                    sh = None
                    for b32 in range(4):
                        row = 0 if b32 < 2 else 1
                        sh = vop("stream_shuffle",
                                 rbc[b32 * 32:(b32 + 1) * 32,
                                     tt * 512:(tt + 1) * 512],
                                 rbrow[0:32, :], [row] * 32,
                                 inc=VE if b32 == 3 else None,
                                 waits=[a4] if b32 == 0 else ())
                if "RBC" in dump and s == 0 and l == 0 and oc == 0:
                    sdep.append(dma("y", dbg["RBC"], rbc, QY, waits=[sh]))
                    barrier()
                isq = oc < 4
                qs_col = (QS8 if isq else KSC)[:, l:l + 1]
                cs_t = CSQ2 if isq else CSK2
                sn_t = SNQ2 if isq else SNK2
                dstT = bfv((B_QT2 + oc * 2048) if isq
                           else (B_KT2 + (oc - 4) * 2048), 2048)
                vop("scalar_tensor_tensor", q2r, q2r, qs_col, rbc,
                    ALU.mult, ALU.mult, waits=[sh])
                vop("tensor_copy", sqb[0:32, :], q2r[32:64, :])
                vop("tensor_copy", sqb[32:64, :], q2r[0:32, :])
                vop("tensor_copy", sqb[64:96, :], q2r[96:128, :])
                vop("tensor_copy", sqb[96:128, :], q2r[64:96, :])
                vop("tensor_mul", q2r, q2r, cs_t)
                vop("tensor_mul", sqb, sqb, sn_t)
                qk_tok = vop("tensor_add", dstT, q2r, sqb, inc=VE)
            qkv2_done = qk_tok

            if "QT2" in dump and s == 0 and l == 0:
                for oc in range(4):
                    sdep.append(dma("y", dbg["QT2"][oc * 128:(oc + 1) * 128, :],
                                    bfv(B_QT2 + oc * 2048, 2048), QY,
                                    waits=[qkv2_done]))
            if dump:
                barrier()
            if "KT2" in dump and s == 0 and l == 0:
                for oc in range(4):
                    sdep.append(dma("y", dbg["KT2"][oc * 128:(oc + 1) * 128, :],
                                    bfv(B_KT2 + oc * 2048, 2048), QY,
                                    waits=[qkv2_done]))
            if dump:
                barrier()

            if "V65L" in dump and s == 0 and l == 0:
                for tc in range(NTC):
                    sdep.append(dma("y", dbg["V65L"][tc * 128:(tc + 1) * 128, :],
                                    bfv(B_V65 + tc * 520, 520), QY,
                                    waits=[v_done]))
            if dump:
                barrier()

            wob = bfv(B_WB, 4 * 1024)
            wold = dma("g", wob.rearrange("p (c m) -> p c m", c=4),
                       di["fwo"][l].rearrange("(c p) m -> p c m", p=128),
                       QW[3], waits=[v_done])
            obc = f32v(B_RBC // 2, 512)
            wo_tok = None
            TRANSF = TRANS.bitcast(F32)
            for w_ in range(4):
                chunks = []
                if w_ > 0:
                    for j in range(4):
                        chunks.append(("U", (w_ - 1) * 4 + j, 0,
                                       (j + 1) * 128, j * 128))
                for j in range(4):
                    chunks.append(("C", w_ * 4 + j, j * 128,
                                   512 - j * 128, j * 128))
                nch = len(chunks)
                pslot_tok = [None] * 6
                pctr = [0]
                m2_ = None
                for pgl in range(2):
                    prs = (2 * pgl, 2 * pgl + 1)
                    sbank = {prs[0]: (0, 1), prs[1]: (2, 3)}
                    obank = {prs[0]: (PS[4], PS[5]), prs[1]: (PS[6], TRANSF)}
                    exp_prev = {p: [None, None] for p in prs}
                    avts = {p: [None, None] for p in prs}
                    meta_prev = {p: None for p in prs}
                    for ci, (kind, kg, qoff, nal, moff) in enumerate(chunks):
                        for p in prs:
                            b0, b1 = sbank[p]
                            wS = [qkv2_done]
                            if wo_tok is not None:
                                wS.append(wo_tok)
                            if bank[b0]["free"] is not None:
                                wS.append(bank[b0]["free"])
                            kslice = bfv(B_KT2 + p * 2048 + kg * 128, 128)
                            qslice = bfv(B_QT2 + p * 2048 + w_ * 512 + qoff,
                                         nal)
                            mm(PS[b0][:, 0:nal], kslice[0:64, :],
                               qslice[0:64, :], start=True, stop=True,
                               waits=wS)
                            wS2 = ([bank[b1]["free"]]
                                   if bank[b1]["free"] is not None else ())
                            t2 = mm(PS[b1][:, 0:nal], kslice[64:128, :],
                                    qslice[64:128, :], start=True, stop=True,
                                    inc=TE, waits=wS2)
                            bank[b0]["ready"] = t2
                            bank[b1]["ready"] = t2
                            msk = MASKC if kind == "C" else MASKU
                            vop("tensor_add",
                                PS[b0][:, moff - qoff:moff - qoff + 128],
                                PS[b0][:, moff - qoff:moff - qoff + 128],
                                msk[:, :], waits=[t2])
                            mk2 = vop("tensor_add",
                                      PS[b1][:, moff - qoff:moff - qoff + 128],
                                      PS[b1][:, moff - qoff:moff - qoff + 128],
                                      msk[:, :], inc=VE)
                            for hh, bx in ((0, b0), (1, b1)):
                                slot = pctr[0] % 6
                                pctr[0] += 1
                                wE = [mk2]
                                if pslot_tok[slot] is not None:
                                    wE.append(pslot_tok[slot])
                                ek = act(bfv(B_PR + slot * 512, nal),
                                         PS[bx][:, 0:nal], AF.Exp, inc=SE,
                                         waits=wE)
                                bank[bx]["free"] = ek
                                exp_cur = (ek, slot)
                                if hh == 0:
                                    exp_c0 = exp_cur
                                else:
                                    exp_c1 = exp_cur
                            if ci > 0:
                                pkind, pkg, pqoff, pnal, pmoff = meta_prev[p]
                                for hh in range(2):
                                    ekp, slot = exp_prev[p][hh]
                                    ob = obank[p][hh]
                                    wA = [ekp]
                                    if ci == 1:
                                        wA.append(v_done)
                                        if p == prs[1]:
                                            fr = (bank[6]["free"] if hh == 0
                                                  else trans_tok["free"])
                                        else:
                                            fr = bank[4 + hh]["free"]
                                        if fr is not None:
                                            wA.append(fr)
                                    avts[p][hh] = mm(
                                        ob[0:65, pqoff:pqoff + pnal],
                                        bfv(B_V65 + pkg * 520 +
                                            (p * 2 + hh) * 65, 65),
                                        bfv(B_PR + slot * 512, pnal),
                                        start=(ci == 1), stop=False,
                                        inc=TE, waits=wA)
                                    pslot_tok[slot] = avts[p][hh]
                            exp_prev[p] = [exp_c0, exp_c1]
                            meta_prev[p] = (kind, kg, qoff, nal, moff)
                    for p in prs:
                        pkind, pkg, pqoff, pnal, pmoff = meta_prev[p]
                        for hh in range(2):
                            ekp, slot = exp_prev[p][hh]
                            avts[p][hh] = mm(
                                obank[p][hh][0:65, pqoff:pqoff + pnal],
                                bfv(B_V65 + pkg * 520 + (p * 2 + hh) * 65, 65),
                                bfv(B_PR + slot * 512, pnal),
                                start=False, stop=True, inc=TE, waits=[ekp])
                            pslot_tok[slot] = avts[p][hh]
                    for p in prs:
                        o0, o1 = obank[p]
                        vop("tensor_copy", rbrow[0:1, :], o0[64:65, :],
                            waits=[avts[p][0]])
                        rc = vop("tensor_copy", rbrow[32:33, :], o1[64:65, :],
                                 inc=VE, waits=[avts[p][1]])
                        act(rbrow[0:33, :], rbrow[0:33, :], AF.Ln, waits=[rc])
                        rc2 = act(rbrow[0:33, :], rbrow[0:33, :], AF.Exp,
                                  scale=-1.0, inc=SE)
                        sh = None
                        for b32 in range(4):
                            srow = 0 if b32 < 2 else 32
                            sh = vop("stream_shuffle",
                                     obc[b32 * 32:(b32 + 1) * 32, :],
                                     rbrow[srow:srow + 32, :], [0] * 32,
                                     inc=VE if b32 == 3 else None,
                                     waits=[rc2] if b32 == 0 else ())
                        vop("tensor_mul", bfv(B_OQL + p * 512, 512)[0:64, :],
                            o0[0:64, :], obc[0:64, :], waits=[sh])
                        m2_ = vop("tensor_mul",
                                  bfv(B_OQL + p * 512, 512)[64:128, :],
                                  o1[0:64, :], obc[64:128, :], inc=VE)
                        if p == prs[0]:
                            bank[4]["free"] = m2_
                            bank[5]["free"] = m2_
                        else:
                            bank[6]["free"] = m2_
                            trans_tok["free"] = m2_
                oql_ready = m2_
                if "OQL" in dump and s == 0 and l == 0 and w_ == 1:
                    for oc in range(4):
                        sdep.append(dma("y",
                                        dbg["OQL"][oc * 128:(oc + 1) * 128, :],
                                        bfv(B_OQL + oc * 512, 512), QY,
                                        waits=[oql_ready]))
                for tc4 in range(4):
                    tc = w_ * 4 + tc4
                    b0 = (tc4 % 2) * 2
                    mtk = None
                    for oc in range(4):
                        for et in range(2):
                            w2 = [oql_ready, wold] if (oc == 0 and et == 0) else ()
                            if oc == 0 and bank[b0 + et]["free"] is not None:
                                w2 = list(w2) + [bank[b0 + et]["free"]]
                            mtk = mm(PS[b0 + et][:, :],
                                     bfv(B_OQL + oc * 512 + tc4 * 128, 128),
                                     bfv(B_WB + oc * 1024 + et * 512, 512),
                                     start=(oc == 0), stop=(oc == 3),
                                     inc=TE if (oc == 3 and et == 1) else None,
                                     waits=w2)
                    wo_tok = mtk
                    a1 = None
                    for et in range(2):
                        b = b0 + et
                        bank[b]["ready"] = mtk
                        a1 = vop("tensor_add",
                                 x_chunk(tc)[:, et * 512:(et + 1) * 512],
                                 PS[b][:, :],
                                 x_chunk(tc)[:, et * 512:(et + 1) * 512],
                                 inc=VE,
                                 waits=[mtk, xload] if et == 0 else ())
                        bank[b]["free"] = a1
            if dump:
                barrier()
            wo_res_done = bank[(3 * 2 + 1) % 4]["free"]

            if "XB" in dump and s == 0 and l == 0:
                for tc in range(NTC):
                    sdep.append(dma("y", dbg["XB"][tc * 128:(tc + 1) * 128, :],
                                    x_chunk(tc), QY, waits=[wo_res_done]))
            if dump:
                barrier()

            w2toks = []
            for ic in range(NJ):
                na = 128 if ic < NJ - 1 else TAILJ
                w2toks.append(dma("g", bfv(M_W2 + ic * 1024, 1024)[0:na, :],
                                  di["fw2"][l, ic * 128:ic * 128 + na, :],
                                  QW[2], waits=[wo_tok] if ic == 0 else ()))
            w2ld = w2toks[-1]
            b2b = f32v(M_B2B // 2, 1024)
            last = wo_tok
            for half in range(2):
                tk1 = dma("y", ARF[0:1, B_RB // 2:B_RB // 2 + 512],
                          di["fb2"][l, half * 512:(half + 1) * 512]
                          .rearrange("(one x) -> one x", one=1), QY,
                          waits=[last])
                for b32 in range(4):
                    last = vop("stream_shuffle",
                               b2b[b32 * 32:(b32 + 1) * 32,
                                   half * 512:(half + 1) * 512],
                               ARF[0:32, B_RB // 2:B_RB // 2 + 512], [0] * 32,
                               inc=VE, waits=[tk1])
            b2b_tok = last

            xt2_tok = rebuild_xt(s, entry=wo_res_done)
            if "XTR" in dump and s == 0 and l == 0:
                for dc in range(8):
                    sdep.append(dma("y", dbg["XTR"][dc * 128:(dc + 1) * 128, :],
                                    xt_block(dc), QY, waits=[xt2_tok]))
                barrier()
            mrow = f32v(L_MBC // 2, 2048)
            rrow = f32v(L_RBC // 2, 2048)
            trow = f32v(L_TMP // 2, 2048)
            sqt8 = ([bfv(L_SQT + i * 512, 512) for i in range(6)] +
                    [bfv(L_SQT2 + i * 512, 512) for i in range(2)])
            sq_tok = [None]
            ev_ = None
            for tt in range(4):
                mtk = None
                for dc in range(8):
                    w0 = [xt2_tok] if (tt == 0 and dc == 0) else ()
                    if dc == 0 and bank[6]["free"] is not None:
                        w0 = list(w0) + [bank[6]["free"]]
                    mtk = mm(PS[6][0:1, :], ONES1[:, :],
                             xt_shift_cols(dc, tt * 512, 512),
                             start=(dc == 0), stop=(dc == 7),
                             inc=TE if dc == 7 else None, waits=w0)
                bank[6]["ready"] = mtk
                ev_ = vop("tensor_copy", mrow[0:1, tt * 512:(tt + 1) * 512],
                          PS[6][0:1, :], inc=VE, waits=[mtk])
                bank[6]["free"] = ev_
                stq = None
                for dc in range(8):
                    wsq = [ev_] if dc == 0 else ()
                    if sq_tok[0] is not None and dc == 0:
                        wsq = list(wsq) + [sq_tok[0]]
                    stq = vop("tensor_mul", sqt8[dc],
                              xt_shift_cols(dc, tt * 512, 512),
                              xt_shift_cols(dc, tt * 512, 512),
                              inc=VE if dc == 7 else None, waits=wsq)
                mtk = None
                for dc in range(8):
                    mtk = mm(PS[6][0:1, :], ONES1[:, :], sqt8[dc],
                             start=(dc == 0), stop=(dc == 7),
                             inc=TE if dc == 7 else None,
                             waits=[stq] if dc == 0 else ())
                sq_tok[0] = mtk
                bank[6]["ready"] = mtk
                ev_ = vop("tensor_copy", rrow[0:1, tt * 512:(tt + 1) * 512],
                          PS[6][0:1, :], inc=VE, waits=[mtk])
                bank[6]["free"] = ev_
            d6 = ev_
            if "LROWS" in dump and s == 0 and l == 0:
                sdep.append(dma("y", dbg["LROWS"][0:1, :], mrow[0:1, :], QY,
                                waits=[d6]))
                sdep.append(dma("y", dbg["LROWS"][1:2, :], rrow[0:1, :], QY,
                                waits=[d6]))
                barrier()
            vop("tensor_scalar_mul", mrow[0:1, :], mrow[0:1, :], 1.0 / D,
                waits=[d6])
            vop("tensor_scalar_mul", rrow[0:1, :], rrow[0:1, :], 1.0 / D)
            vop("tensor_mul", trow[0:1, :], mrow[0:1, :], mrow[0:1, :])
            vop("tensor_sub", rrow[0:1, :], rrow[0:1, :], trow[0:1, :])
            rv = vop("tensor_scalar_add", rrow[0:1, :], rrow[0:1, :], EPS_LN,
                     inc=VE)
            act(rrow[0:1, :], rrow[0:1, :], AF.Ln, waits=[rv])
            re = act(rrow[0:1, :], rrow[0:1, :], AF.Exp, scale=-0.5, inc=SE)
            sh = None
            for b32 in range(4):
                sh = vop("stream_shuffle",
                         mrow[b32 * 32:(b32 + 1) * 32, :], mrow[0:32, :],
                         [0] * 32, waits=[re] if b32 == 0 else ())
            for b32 in range(4):
                sh = vop("stream_shuffle",
                         rrow[b32 * 32:(b32 + 1) * 32, :], rrow[0:32, :],
                         [0] * 32, inc=VE if b32 == 3 else None)
            if "LMU" in dump and s == 0 and l == 0:
                sdep.append(dma("y", dbg["LMU"], mrow, QY, waits=[sh]))
                sdep.append(dma("y", dbg["LRS"], rrow, QY, waits=[sh]))
                barrier()
            trow2 = f32v(B_XBN // 2, 2048)
            hn_toks = [None] * 8
            for dc in range(8):
                tb = trow if dc % 2 == 0 else trow2
                wt1 = [sh] if dc == 0 else []
                if dc >= 2:
                    wt1.append(hn_toks[dc - 2])
                t1 = vop("tensor_sub", tb, xt_shift_cols(dc, 0, 2048), mrow,
                         waits=wt1)
                t2 = vop("tensor_mul", tb, tb, rrow, inc=VE)
                hn_toks[dc] = act(xt_block(dc)[:, 1:2049], tb, AF.Identity,
                                  scale=LNG[:, l * 8 + dc:l * 8 + dc + 1],
                                  bias=LNB[:, l * 8 + dc:l * 8 + dc + 1],
                                  inc=SE, waits=[t2])
            hn_tok = hn_toks[7]
            if "HN" in dump and s == 0 and l == 0:
                for dc in range(8):
                    sdep.append(dma("y", dbg["HN"][dc * 128:(dc + 1) * 128, :],
                                    xt_block(dc), QY, waits=[hn_tok]))
            if dump:
                barrier()

            gsc = f32v(M_GSC // 2, 512)
            wag_tok = [None, None]
            wag_base = [M_WAG, 74112]
            gel_tok = None
            for q4 in range(4):
                mj_toks = []
                for jp in range(NJ // 2):
                    j0 = 2 * jp
                    m_ = 256 if jp < NJ // 2 - 1 else 128 + TAILJ
                    sl = jp % 2
                    base = wag_base[sl]
                    wag = bfv(base, 2 * 8 * m_)
                    wwl = [hn_tok]
                    if wag_tok[sl] is not None:
                        wwl.append(wag_tok[sl])
                    wld = dma("g",
                              wag.rearrange(
                                  "p (two c m) -> p two c m", two=2, c=8),
                              di["fw1"][l].rearrange(
                                  "(c p) (two r) -> p two c r", p=128, two=2)
                              [:, :, :, j0 * 128:j0 * 128 + m_], QW[sl],
                              waits=wwl)
                    for k in range(2):
                        j = j0 + k
                        na = 128 if j < NJ - 1 else TAILJ
                        bA = j % 2
                        bG = 2 + j % 2
                        mtk = None
                        for dc in range(8):
                            w2_ = [wld] if dc == 0 else ()
                            if dc == 0 and bank[bA]["free"] is not None:
                                w2_ = list(w2_) + [bank[bA]["free"]]
                            mtk = mm(PS[bA][0:na, :],
                                     bfv(base + dc * m_ + k * 128, na),
                                     xt_block(dc)[:, 1 + q4 * 512:
                                                  1 + (q4 + 1) * 512],
                                     start=(dc == 0), stop=(dc == 7),
                                     inc=TE if dc == 7 else None, waits=w2_)
                        bank[bA]["ready"] = mtk
                        mtk = None
                        for dc in range(8):
                            w2_ = () if dc else ([bank[bG]["free"]]
                                                 if bank[bG]["free"] is not None
                                                 else ())
                            mtk = mm(PS[bG][0:na, :],
                                     bfv(base + 8 * m_ + dc * m_ + k * 128, na),
                                     xt_block(dc)[:, 1 + q4 * 512:
                                                  1 + (q4 + 1) * 512],
                                     start=(dc == 0), stop=(dc == 7),
                                     inc=TE, waits=w2_)
                        bank[bG]["ready"] = mtk
                        wg = [bank[bG]["ready"]]
                        if gel_tok is not None:
                            wg.append(gel_tok)
                        gel_tok = act(gsc[0:na, :], PS[bG][0:na, :], AF.Gelu,
                                      bias=B1G[0:na, l * NJ + j:l * NJ + j + 1],
                                      inc=SE, waits=wg)
                        mj = vop("scalar_tensor_tensor",
                                 bfv(M_M + j * 512, 512)[0:na, :],
                                 PS[bA][0:na, :],
                                 B1A[0:na, l * NJ + j:l * NJ + j + 1],
                                 gsc[0:na, :], ALU.add, ALU.mult, inc=VE,
                                 waits=[bank[bA]["ready"], gel_tok])
                        bank[bA]["free"] = mj
                        bank[bG]["free"] = mj
                        mj_toks.append(mj)
                    wag_tok[sl] = mj_toks[-1]
                m_done = mj_toks[-1]
                if "M" in dump and s == 0 and l == 0 and q4 == 0:
                    for j in range(NJ):
                        sdep.append(dma("y", dbg["M"][j * 128:(j + 1) * 128, :],
                                        bfv(M_M + j * 512, 512), QY,
                                        waits=[m_done]))
                TRANSF2 = TRANS.bitcast(F32)
                for tc4 in range(4):
                    tc = q4 * 4 + tc4
                    pb_ = ((PS[4], PS[5]) if tc4 % 2 == 0
                           else (PS[6], TRANSF2))
                    bk_ = ((4, 5) if tc4 % 2 == 0 else (6, None))
                    mtk = None
                    for ic in range(NJ):
                        na = 128 if ic < NJ - 1 else TAILJ
                        for et in range(2):
                            w2_ = [m_done, w2ld] if (ic == 0 and et == 0) else ()
                            if ic == 0:
                                fr = (bank[bk_[et]]["free"] if bk_[et] is not None
                                      else trans_tok["free"])
                                if fr is not None:
                                    w2_ = list(w2_) + [fr]
                            mtk = mm(pb_[et][:, :],
                                     bfv(M_M + ic * 512 + tc4 * 128, 128)
                                     [0:na, :],
                                     bfv(M_W2 + ic * 1024 + et * 512, 512)
                                     [0:na, :],
                                     start=(ic == 0), stop=(ic == NJ - 1),
                                     inc=TE if (ic == NJ - 1 and et == 1)
                                     else None, waits=w2_)
                    a2_ = None
                    for et in range(2):
                        xsl = x_chunk(tc)[:, et * 512:(et + 1) * 512]
                        vop("tensor_add", xsl, pb_[et][:, :], xsl,
                            waits=[mtk, b2b_tok] if et == 0 else ())
                        a2_ = vop("tensor_add", xsl, xsl,
                                  b2b[:, et * 512:(et + 1) * 512], inc=VE)
                        if bk_[et] is not None:
                            bank[bk_[et]]["free"] = a2_
                        else:
                            trans_tok["free"] = a2_
            if dump:
                barrier()
            if "XB2" in dump and s == 0 and l == 0:
                for tc in range(NTC):
                    sdep.append(dma("y", dbg["XB2"][tc * 128:(tc + 1) * 128, :],
                                    x_chunk(tc), QY, waits=[a2_]))
            if dump:
                barrier()
            return sdep

        def phaseF(s):
            sdep = []
            barrier()
            def fbuf(qt):
                return (f32v(M_XPW // 2, 4096) if qt % 2 == 0
                        else f32v(24576, 4096))

            xlt = {}
            odt = {}

            def loadq(qt, waits):
                b = fbuf(qt)
                xlt[qt] = dma("y",
                              b[:, 0:4096].rearrange("p (c m) -> p c m", c=4),
                              xpa[qt * 512:(qt + 1) * 512, :]
                              .rearrange("(c p) m -> p c m", p=128),
                              QX, waits=waits)

            loadq(0, [])
            loadq(1, [])
            for qt in range(4):
                xpw = fbuf(qt)
                xl = xlt[qt]
                adds = None
                for t4 in range(4):
                    tc = qt * 4 + t4
                    adds = vop("tensor_add", xpw[:, t4 * 1024:(t4 + 1) * 1024],
                               xpw[:, t4 * 1024:(t4 + 1) * 1024], x_chunk(tc),
                               inc=VE if t4 == 3 else None,
                               waits=[xl] if t4 == 0 else ())
                xws = [xpw[:, t4 * 1024:(t4 + 1) * 1024] for t4 in range(4)]
                SUMc = [STAT[:, 4 * t4:4 * t4 + 1] for t4 in range(4)]
                VARc = [STAT[:, 16 + 4 * t4:17 + 4 * t4] for t4 in range(4)]
                M2c = [STAT[:, 32 + 4 * t4:33 + 4 * t4] for t4 in range(4)]
                for t4 in range(4):
                    vop("reduce_sum", SUMc[t4], xws[t4], AX.X,
                        waits=[adds] if t4 == 0 else ())
                for t4 in range(4):
                    vop("tensor_mul", f32v(L_TMP // 2, 1024), xws[t4], xws[t4])
                    vop("reduce_sum", VARc[t4], f32v(L_TMP // 2, 1024), AX.X)
                for t4 in range(4):
                    vop("tensor_scalar", M2c[t4], SUMc[t4], SUMc[t4],
                        EPS_LN * D * D, ALU.mult, ALU.subtract)
                for t4 in range(4):
                    vop("tensor_scalar_mul", M2c[t4], M2c[t4], 1.0 / (D * D))
                for t4 in range(4):
                    vop("tensor_scalar_mul", SUMc[t4], SUMc[t4], 1.0 / D)
                for t4 in range(4):
                    vop("tensor_scalar_mul", VARc[t4], VARc[t4], 1.0 / D)
                v3 = None
                for t4 in range(4):
                    v3 = vop("tensor_sub", VARc[t4], VARc[t4], M2c[t4],
                             inc=VE if t4 == 3 else None)
                for t4 in range(4):
                    act(VARc[t4], VARc[t4], AF.Ln,
                        waits=[v3] if t4 == 0 else ())
                l2 = None
                for t4 in range(4):
                    l2 = act(VARc[t4], VARc[t4], AF.Exp, scale=-0.5,
                             inc=SE if t4 == 3 else None)
                vop("tensor_copy", f32v(L_TMP // 2, 512),
                    xws[0][:, 0:512], waits=[l2])
                n1 = None
                for t4 in range(4):
                    n1 = vop("tensor_scalar", xws[t4], xws[t4], SUMc[t4],
                             VARc[t4], ALU.subtract, ALU.mult,
                             inc=VE if t4 == 3 else None)
                ot = dma("y", out[s, qt * 512:(qt + 1) * 512, :]
                         .rearrange("(c p) m -> p c m", p=128),
                         xpw[:, 0:4096].rearrange("p (c m) -> p c m", c=4),
                         QY, waits=[n1])
                sdep.append(ot)
                odt[qt] = ot
                if qt + 2 < 4:
                    loadq(qt + 2, [ot])
            return sdep

        final_toks = []
        for s in range(NSEQ):
            toks, sdep = phaseA(s)
            final_toks += sdep
            final_toks += phaseB(s, 0)
            final_toks += phaseB(s, 1)
            final_toks += phaseF(s)

        for tk in final_toks:
            P.wait("y", tk)

        with nc.Block() as blk:
            @blk.gpsimd
            def _(g):
                for fn in P.q["g"]:
                    fn(g)

            @blk.sync
            def _(sy):
                for fn in P.q["y"]:
                    fn(sy)

            @blk.tensor
            def _(t):
                for fn in P.q["t"]:
                    fn(t)

            @blk.vector
            def _(v):
                for fn in P.q["v"]:
                    fn(v)

            @blk.scalar
            def _(s_):
                for fn in P.q["s"]:
                    fn(s_)
    return nc


def kernel(L0_attn_x, L0_attn_qkv_w, L0_attn_qkv_b, L0_attn_proj_w, L0_attn_proj_b,
           L0_ffn_wqkv, L0_ffn_q_scale, L0_ffn_k_scale, L0_ffn_wo, L0_ffn_ln_g,
           L0_ffn_ln_b, L0_ffn_w1, L0_ffn_b1, L0_ffn_w2, L0_ffn_b2):
    consts = make_consts()
    nc = build_core()
    common = {
        "qkvw": np.ascontiguousarray(L0_attn_qkv_w, dtype=np.float32),
        "qkvb": np.ascontiguousarray(L0_attn_qkv_b, dtype=np.float32),
        "projw": np.ascontiguousarray(L0_attn_proj_w, dtype=np.float32),
        "projb": np.ascontiguousarray(L0_attn_proj_b, dtype=np.float32),
        "fwqkv": np.ascontiguousarray(L0_ffn_wqkv, dtype=np.float32),
        "fqs": np.ascontiguousarray(L0_ffn_q_scale, dtype=np.float32),
        "fks": np.ascontiguousarray(L0_ffn_k_scale, dtype=np.float32),
        "fwo": np.ascontiguousarray(L0_ffn_wo, dtype=np.float32),
        "flng": np.ascontiguousarray(L0_ffn_ln_g, dtype=np.float32),
        "flnb": np.ascontiguousarray(L0_ffn_ln_b, dtype=np.float32),
        "fw1": np.ascontiguousarray(L0_ffn_w1, dtype=np.float32),
        "fb1": np.ascontiguousarray(L0_ffn_b1, dtype=np.float32),
        "fw2": np.ascontiguousarray(L0_ffn_w2, dtype=np.float32),
        "fb2": np.ascontiguousarray(L0_ffn_b2, dtype=np.float32),
    }
    common.update(consts)
    x = np.ascontiguousarray(L0_attn_x, dtype=np.float32)
    in_maps = []
    for c in range(NCORE):
        m = dict(common)
        m["xin"] = x[c * NSEQ:(c + 1) * NSEQ]
        in_maps.append(m)
    res = run_bass_kernel_spmd(nc, in_maps, core_ids=list(range(NCORE)))
    return np.concatenate([r["out"] for r in res.results], axis=0)
